# version 12
# baseline (speedup 1.0000x reference)
import contextlib
import math
import numpy as np
import concourse.bass as bass
import concourse.mybir as mybir
from concourse.bass_utils import run_bass_kernel_spmd

F32 = mybir.dt.float32
BF16 = mybir.dt.bfloat16
I32 = mybir.dt.int32
AF = mybir.ActivationFunctionType
ALU = mybir.AluOpType
AX = mybir.AxisListType

D = 1024
DK = 1024
DV = 2048
NH = 4
FF = 4096
LR = 16
T = 512
NTILES = 8
NTOK = NTILES * T
NEGLN16 = -math.log(16.0)
EPS = 1e-5

ENGINES = ("sync", "scalar", "vector", "tensor", "gpsimd")
STAGE = 99.0
PUMP = 32
BGN = 10**9


class Plan:

    def __init__(self):
        self.ops = {e: [] for e in ENGINES}
        self.counts = {}
        self.watermark = {}
        self.writer = {}
        self.readers = {}
        self.sem_names = []
        self.defer = False
        self.bgq = []

    def _sem(self, name):
        if name not in self.counts:
            self.counts[name] = 0
            self.sem_names.append(name)
        return name

    def _edges(self, ename, rd, wr, skip_war=False):
        need = {}
        def add(tok):
            s, v = tok
            if v > need.get(s, 0):
                need[s] = v
        for b in rd:
            t = self.writer.get(b)
            if t:
                add(t)
        for b in wr:
            if not skip_war:
                t = self.writer.get(b)
                if t:
                    add(t)
                for rt in self.readers.get(b, ()):
                    add(rt)
        waits = []
        for s, v in need.items():
            if self.watermark.get((ename, s), 0) < v:
                self.watermark[(ename, s)] = v
                waits.append((s, v))
        return waits

    def _post(self, tok, rd, wr):
        for b in wr:
            self.writer[b] = tok
            self.readers[b] = []
        for b in rd:
            self.readers.setdefault(b, []).append(tok)

    def op(self, ename, emit, rd=(), wr=()):
        if self.defer:
            self.bgq.append(("op", (ename, emit, rd, wr)))
            return
        waits = self._edges(ename, rd, wr)
        s = self._sem("e_" + ename)
        self.counts[s] += 1
        self.ops[ename].append((waits, emit, (s, 1)))
        self._post((s, self.counts[s]), rd, wr)

    def mm_group(self, emits, rd=(), wr=()):
        if self.defer:
            self.bgq.append(("mm", (emits, rd, wr)))
            return
        waits = self._edges("tensor", rd, wr)
        s = self._sem("e_tensor")
        self.counts[s] += 1
        n = len(emits)
        for i, em in enumerate(emits):
            self.ops["tensor"].append(
                (waits if i == 0 else [], em, (s, 1) if i == n - 1 else None))
        self._post((s, self.counts[s]), rd, wr)

    def dma(self, ename, emit, rd=(), wr=(), chan=None, append=False):
        if self.defer:
            self.bgq.append(("dma", (ename, emit, rd, wr, chan, append)))
            return
        waits = self._edges(ename, rd, wr, skip_war=append)
        chan = self._sem("d_" + (chan or (wr[0] if wr else rd[0])))
        self.counts[chan] += 16
        self.ops[ename].append((waits, emit, (chan, 16)))
        self._post((chan, self.counts[chan]), rd, wr)

    def pump(self, n):
        assert not self.defer
        if not hasattr(self, "npumped"):
            self.npumped = 0
        n = min(n, BGN - self.npumped)
        if n <= 0:
            return
        self.npumped += n
        for _ in range(min(n, len(self.bgq))):
            kind, args = self.bgq.pop(0)
            if kind == "op":
                self.op(*args)
            elif kind == "mm":
                self.mm_group(*args)
            else:
                ename, emit, rd, wr, chan, append = args
                self.dma(ename, emit, rd=rd, wr=wr, chan=chan, append=append)

    def flush_bg(self):
        while self.bgq:
            kind, args = self.bgq.pop(0)
            if kind == "op":
                self.op(*args)
            elif kind == "mm":
                self.mm_group(*args)
            else:
                ename, emit, rd, wr, chan, append = args
                self.dma(ename, emit, rd=rd, wr=wr, chan=chan, append=append)

    def inherit(self, dst_bufs, src_bufs):
        toks = []
        for b in src_bufs:
            if b in self.writer:
                toks.append(self.writer[b])
            toks.extend(self.readers.get(b, ()))
        for b in dst_bufs:
            for t in toks:
                self.readers.setdefault(b, []).append(t)

    def barrier(self, ename):
        waits = []
        for s in self.sem_names:
            v = self.counts[s]
            if v > 0 and self.watermark.get((ename, s), 0) < v:
                self.watermark[(ename, s)] = v
                waits.append((s, v))
        self.ops[ename].append((waits, None, None))

    def emit(self, nc, es, block):
        sems = {s: es.enter_context(nc.semaphore(s)) for s in self.sem_names}
        for ename in ENGINES:
            oplist = self.ops[ename]
            if not oplist:
                continue
            def body(eng, oplist=oplist):
                for waits, emitfn, inc in oplist:
                    for s, v in waits:
                        eng.wait_ge(sems[s], v)
                    if emitfn is not None:
                        inst = emitfn(eng)
                        if inc is not None:
                            inst.then_inc(sems[inc[0]], inc[1])
            getattr(block, ename)(body)


def build_program(n_tiles=NTILES, n_b_tiles=None, skip_b=False):
    if n_b_tiles is None:
        n_b_tiles = n_tiles
    nc = bass.Bass("TRN2", target_bir_lowering=False)
    P = Plan()

    xin = nc.dram_tensor("xin", [NTOK, D], F32, kind="ExternalInput").ap()
    w = {}
    for name, shape in [
        ("Wq", [D, DK]), ("Wk", [D, DK]), ("Wv", [D, DV]), ("Wg", [D, DV]),
        ("Wgk1", [D, LR]), ("Wgk2", [LR, DK]), ("bgk2", [1, DK]),
        ("gnw", [1, 512]), ("Wo", [DV, D]),
        ("Wfc", [FF, D]), ("bfc", [1, FF]), ("Afc", [LR, D]), ("Bfc", [FF, LR]),
        ("Wpj", [D, FF]), ("bpj", [1, D]), ("Apj", [LR, FF]), ("Bpj", [D, LR]),
    ]:
        w[name] = nc.dram_tensor(name, shape, F32, kind="ExternalInput").ap()
    kb = nc.dram_tensor("kb", [128, 512], BF16, kind="ExternalInput").ap()
    kfc = nc.dram_tensor("kfc", [128, 8], F32, kind="ExternalInput").ap()
    yout = nc.dram_tensor("y", [NTOK, D], F32, kind="ExternalOutput").ap()

    wq_s = nc.dram_tensor("wq_s", [128, 8 * 1024], BF16, kind="Internal").ap()
    wk_s = nc.dram_tensor("wk_s", [128, 8 * 1024], BF16, kind="Internal").ap()
    wv_s = nc.dram_tensor("wv_s", [128, 16384], BF16, kind="Internal").ap()
    wg_s = nc.dram_tensor("wg_s", [128, 16384], BF16, kind="Internal").ap()
    wo_s = nc.dram_tensor("wo_s", [128, 16384], BF16, kind="Internal").ap()
    wfc_s = nc.dram_tensor("wfc_s", [128, 32768], BF16, kind="Internal").ap()
    wpj_s = nc.dram_tensor("wpj_s", [128, 32768], BF16, kind="Internal").ap()
    wfcq_s = nc.dram_tensor("wfcq_s", [128, 32768], BF16, kind="Internal").ap()
    wpjq_s = nc.dram_tensor("wpjq_s", [128, 32768], BF16, kind="Internal").ap()
    x1_s = nc.dram_tensor("x1_s", [NTOK, D], F32, kind="Internal").ap()
    x1T_s = nc.dram_tensor("x1T_s", [128, 8 * 4096], BF16, kind="Internal").ap()

    xv = xin.rearrange("(n p) c -> p n c", p=128)
    yv = yout.rearrange("(n p) c -> p n c", p=128)
    x1v = x1_s.rearrange("(n p) c -> p n c", p=128)

    es = contextlib.ExitStack()
    with es:
        S_ = lambda name, shape, dt=BF16: es.enter_context(nc.sbuf_tensor(name, shape, dt))
        xtm = S_("xtm", [128, 4096], F32)
        x1r = S_("x1r", [128, 1024], F32)
        x1c = S_("x1c", [128, 1024], F32)
        spf = S_("spf", [128, 1024], F32)
        x2f32 = S_("x2f32", [128, 4096], F32)
        cols = S_("cols", [128, 64], F32)
        stats = S_("stats", [128, 32], F32)
        sqs = S_("sqs", [128, 1024])
        sgn = S_("sgn", [128, 1024])
        qi32 = S_("qi32", [128, 512], I32)
        qfp = S_("qfp", [128, 512], F32)
        qbf = S_("qbf", [128, 1024])
        qbf2 = S_("qbf2", [128, 1024])
        xT = S_("xT", [128, 4096])
        qf = S_("qf", [128, 4096])
        kfm = S_("kfm", [128, 4096])
        meg = S_("meg", [128, 32768])
        sphl = S_("sphl", [128, 2048])
        eqc = S_("eqc", [128, 1024])
        ekc = S_("ekc", [128, 1024])
        erc = S_("erc", [128, 1024])
        qg = S_("qg", [128, 1024])
        ki = S_("ki", [128, 1024])
        kt = S_("kt", [128, 1024])
        St = S_("St", [128, 4096])
        scm = S_("scm", [128, 512])
        og = S_("og", [128, 2048])
        x1bf = S_("x1bf", [128, 2048])
        x1Tt = S_("x1Tt", [128, 4096])
        diagd = S_("diagd", [128, 1024])
        wgk1p = S_("wgk1p", [128, 1024])
        wgk2a = S_("wgk2a", [128, 1024])
        afcT = S_("afcT", [128, 128])
        apjT = S_("apjT", [128, 512])
        fca = S_("fca", [128, 4096])
        pja = S_("pja", [128, 1024])
        t1a = S_("t1a", [128, 512])
        za = S_("za", [128, 512])
        z2a = S_("z2a", [128, 512])
        id128 = S_("id128", [128, 128])
        idf32 = S_("idf32", [128, 128], F32)
        ones1 = S_("ones1", [128, 1])
        Ub = S_("Ubm", [128, 128])
        Lb = S_("Lbm", [128, 128])
        mskT = S_("mskT", [128, 128])

        pq = es.enter_context(nc.psum_tensor("pq", [128, 1024], F32))
        pr = es.enter_context(nc.psum_tensor("pr", [128, 1024], F32))
        po = es.enter_context(nc.psum_tensor("po", [128, 1024], F32))
        pu = es.enter_context(nc.psum_tensor("pu", [128, 512], F32))
        pt = es.enter_context(nc.psum_tensor("pt", [128, 1024], BF16))

        halves = {"pq0": pq[:, 0:512], "pq1": pq[:, 512:1024],
                  "pr0": pr[:, 0:512], "pr1": pr[:, 512:1024],
                  "po0": po[:, 0:512], "po1": po[:, 512:1024]}
        vt = meg[:, 0:8192]
        gs = meg[:, 8192:16384]
        ar = [meg[:, 16384 + i * 4096: 16384 + (i + 1) * 4096] for i in range(4)]
        h1 = meg[:, 0:16384]
        AR = ["ar0", "ar1", "ar2", "ar3"]
        H1K = ["h1_%d" % o for o in range(32)]

        C_PROBE, C_NLN16, C_EPS, C_GNW, C_HALF = 0, 1, 2, 3, 7
        C_RCPF, C_SF, C_R2F, C_RCPP, C_SP, C_R2P = 8, 9, 10, 11, 12, 13
        C_T0, C_T1, C_T2 = 14, 15, 16
        C_DC = 24
        C_SSQ = 32
        C_R = 40
        C_LN = 44

        P.dma("sync", lambda e: e.dma_start(out=id128[:], in_=kb[:, 0:128]), wr=["id128"])
        P.dma("sync", lambda e: e.dma_start(out=Ub[:], in_=kb[:, 128:256]), wr=["Ub"])
        P.dma("sync", lambda e: e.dma_start(out=Lb[:], in_=kb[:, 256:384]), wr=["Lb"])
        P.dma("sync", lambda e: e.dma_start(out=mskT[:], in_=kb[:, 384:512]), wr=["mskT"])
        P.dma("sync", lambda e: e.dma_start(out=cols[:, 0:8], in_=kfc[:, :]), wr=["cols"])
        P.dma("sync", lambda e: e.dma_start(
            out=cols[:, C_GNW:C_GNW + 4],
            in_=w["gnw"].rearrange("o (a p) -> p o a", p=128)[:, 0, :],
            allow_slow_non_contiguous=True),
            rd=["cols"], wr=["cols"], chan="cgnw")
        P.op("vector", lambda e: e.tensor_copy(out=idf32[:], in_=id128[:]),
             rd=["id128"], wr=["idf32"])
        P.op("vector", lambda e: e.memset(ones1[:], 1.0), wr=["ones1"])
        P.op("vector", lambda e: e.tensor_copy(out=qi32[:, 0:1], in_=cols[:, 0:1]),
             rd=["cols"], wr=["qi32"])
        P.op("vector", lambda e: e.tensor_copy(out=cols[:, C_T0:C_T0 + 1], in_=qi32[:, 0:1]),
             rd=["qi32"], wr=["cols"])
        P.op("vector", lambda e: e.tensor_scalar(
            out=cols[:, C_HALF:C_HALF + 1], in0=cols[:, C_T0:C_T0 + 1],
            scalar1=-0.5, op0=ALU.mult, scalar2=1.0, op1=ALU.add),
            rd=["cols"], wr=["cols"])

        P.dma("gpsimd", lambda e: e.dma_start(
            out=wq_s.rearrange("p (o k c) -> p o k c", k=8, c=128),
            in_=w["Wq"].rearrange("(k p) (o c) -> p o k c", p=128, c=128)),
            wr=["wq_s"], chan="wq_s")
        P.dma("gpsimd", lambda e: e.dma_start(
            out=wk_s.rearrange("p (o k c) -> p o k c", k=8, c=128),
            in_=w["Wk"].rearrange("(k p) (o c) -> p o k c", p=128, c=128)),
            wr=["wk_s"], chan="wk_s")
        P.dma("gpsimd", lambda e: e.dma_start(
            out=wv_s.rearrange("p (oh k c) -> p oh k c", k=8, c=512),
            in_=w["Wv"].rearrange("(k p) (oh c) -> p oh k c", p=128, c=512)),
            wr=["wv_s"], chan="wv_s")
        P.dma("gpsimd", lambda e: e.dma_start(
            out=wg_s.rearrange("p (oh k c) -> p oh k c", k=8, c=512),
            in_=w["Wg"].rearrange("(k p) (oh c) -> p oh k c", p=128, c=512)),
            wr=["wg_s"], chan="wg_s")
        P.dma("gpsimd", lambda e: e.dma_start(
            out=wo_s.rearrange("p (oh k c) -> p oh k c", k=16, c=512),
            in_=w["Wo"].rearrange("(k p) (oh c) -> p oh k c", p=128, c=512)),
            wr=["wo_s"], chan="wo_s")

        P.op("vector", lambda e: e.memset(wgk1p[:], 0.0), wr=["wgk1p"])
        P.dma("sync", lambda e: e.dma_start(
            out=xtm[:, 0:8 * 16].rearrange("p (a c) -> p a c", c=16),
            in_=w["Wgk1"].rearrange("(a p) c -> p a c", p=128)),
            rd=["xtm"], wr=["xtm"], chan="sml")
        for k in range(8):
            P.op("vector", lambda e, k=k: e.tensor_copy(
                out=wgk1p[:, k * 128:k * 128 + 16], in_=xtm[:, k * 16:(k + 1) * 16]),
                rd=["xtm"], wr=["wgk1p"])
        P.op("vector", lambda e: e.memset(wgk2a[:], 0.0), wr=["wgk2a"])
        P.dma("sync", lambda e: e.dma_start(out=xtm[0:16, 0:1024], in_=w["Wgk2"][:, :]),
              rd=["xtm"], wr=["xtm"], chan="sml")
        P.op("vector", lambda e: e.tensor_copy(
            out=wgk2a[0:16, 0:1024], in_=xtm[0:16, 0:1024]), rd=["xtm"], wr=["wgk2a"])
        P.dma("sync", lambda e: e.dma_start(out=xtm[0:1, 0:1024], in_=w["bgk2"][:, :]),
              rd=["xtm"], wr=["xtm"], chan="sml")
        P.op("vector", lambda e: e.tensor_copy(
            out=sqs[0:1, 0:1024], in_=xtm[0:1, 0:1024]), rd=["xtm"], wr=["sqs"])
        P.dma("sync", lambda e: e.dma_start(
            out=wgk2a[16:17, 0:1024], in_=sqs[0:1, 0:1024]),
            rd=["sqs"], wr=["wgk2a"], chan="c_wgk2a")
        P.op("vector", lambda e: e.memset(sqs[0:1, 0:512], 1.0), wr=["sqs"])
        for key, t in (("t1a", t1a), ("za", za), ("z2a", z2a)):
            P.op("vector", lambda e, t=t: e.memset(t[:], 0.0), wr=[key])
            P.dma("sync", lambda e, t=t: e.dma_start(
                out=t[16:17, :], in_=sqs[0:1, 0:512]),
                rd=["sqs"], wr=[key], chan="c_" + key)

        def small_T(hbm, ncols, dst, dstkey):
            P.dma("gpsimd", lambda e: e.dma_start(out=x2f32[0:16, 0:ncols], in_=hbm[:, :]),
                  rd=["x2f32"], wr=["x2f32"], chan="qwl")
            P.op("vector", lambda e: e.tensor_copy(
                out=qbf[0:16, 0:ncols if ncols <= 1024 else 1024],
                in_=x2f32[0:16, 0:ncols if ncols <= 1024 else 1024]),
                rd=["x2f32"], wr=["qbf"])
            nb = ncols // 128
            done = 0
            while done < nb:
                if ncols > 1024 and done * 128 % 1024 == 0 and done > 0:
                    off = done * 128
                    P.op("vector", lambda e, off=off: e.tensor_copy(
                        out=qbf[0:16, 0:1024], in_=x2f32[0:16, off:off + 1024]),
                        rd=["x2f32"], wr=["qbf"])
                cnt = min(8, nb - done, (1024 - (done * 128 % 1024)) // 128)
                base = done * 128 % 1024
                P.mm_group([
                    (lambda e, kk=kk, base=base: e.transpose(
                        out=pt[:, kk * 16:(kk + 1) * 16],
                        in_=qbf[0:16, base + kk * 128:base + (kk + 1) * 128],
                        identity=id128[0:16, 0:16]))
                    for kk in range(cnt)], rd=["qbf", "id128"], wr=["pt"])
                P.op("vector", lambda e, done=done, cnt=cnt: e.tensor_copy(
                    out=dst[:, done * 16:(done + cnt) * 16], in_=pt[:, 0:cnt * 16]),
                    rd=["pt"], wr=[dstkey])
                done += cnt

        P.defer = True
        small_T(w["Afc"], 1024, afcT, "afcT")
        small_T(w["Apj"], 4096, apjT, "apjT")

        def quant_weight(hbm, n_out, n_in, qnat, qkey, c_rcp, c_s, c_r2):
            no, ni = n_out // 128, n_in // 128
            hv = hbm.rearrange("(a p) c -> p a c", p=128)
            for o in range(no):
                P.dma("gpsimd", lambda e, o=o: e.dma_start(
                    out=x2f32[:, 0:n_in], in_=hv[:, o, :]), wr=["x2f32"], chan="qwl")
                P.op("vector", lambda e, o=o: e.tensor_reduce(
                    out=stats[:, o:o + 1], in_=x2f32[:, 0:n_in], op=ALU.max, axis=AX.X,
                    apply_absolute_value=True), rd=["x2f32"], wr=["stats"])
            P.op("vector", lambda e, no=no: e.tensor_reduce(
                out=cols[:, C_T1:C_T1 + 1], in_=stats[:, 0:no], op=ALU.max, axis=AX.X),
                rd=["stats"], wr=["cols"])
            P.dma("sync", lambda e: e.dma_start(
                out=x2f32[0:1, 0:128], in_=cols[:, C_T1:C_T1 + 1],
                allow_slow_non_contiguous=True),
                rd=["cols"], wr=["x2f32"], chan="c_amax")
            P.op("vector", lambda e: e.tensor_reduce(
                out=cols[0:1, C_T0:C_T0 + 1], in_=x2f32[0:1, 0:128], op=ALU.max,
                axis=AX.X), rd=["x2f32"], wr=["cols"])
            P.op("vector", lambda e: e.memset(qfp[0:1, 0:128], 1.0), wr=["qfp"])
            P.op("vector", lambda e: e.tensor_scalar(
                out=x2f32[0:1, 0:128], in0=qfp[0:1, 0:128],
                scalar1=cols[0:1, C_T0:C_T0 + 1], op0=ALU.mult, scalar2=None),
                rd=["qfp", "cols"], wr=["x2f32"])
            P.dma("sync", lambda e: e.dma_start(
                out=cols[:, C_T2:C_T2 + 1], in_=x2f32[0:1, 0:128],
                allow_slow_non_contiguous=True),
                rd=["x2f32"], wr=["cols"], chan="c_amax")
            P.op("vector", lambda e: e.tensor_scalar(
                out=cols[:, c_s:c_s + 1], in0=cols[:, C_T2:C_T2 + 1],
                scalar1=1.0 / 127.0, op0=ALU.mult, scalar2=None), rd=["cols"], wr=["cols"])
            P.op("vector", lambda e: e.reciprocal(
                out=cols[:, C_T0:C_T0 + 1], in_=cols[:, C_T2:C_T2 + 1]),
                rd=["cols"], wr=["cols"])
            P.op("vector", lambda e: e.tensor_scalar(
                out=cols[:, c_rcp:c_rcp + 1], in0=cols[:, C_T0:C_T0 + 1],
                scalar1=127.0, op0=ALU.mult, scalar2=None), rd=["cols"], wr=["cols"])
            P.op("vector", lambda e: e.tensor_scalar(
                out=cols[:, c_r2:c_r2 + 1], in0=cols[:, C_T0:C_T0 + 1],
                scalar1=254.0, op0=ALU.mult, scalar2=None), rd=["cols"], wr=["cols"])
            for o in range(no):
                P.dma("gpsimd", lambda e, o=o: e.dma_start(
                    out=x2f32[:, 0:n_in], in_=hv[:, o, :]), wr=["x2f32"], chan="qwl")
                for cg in range(0, n_in, 512):
                    half = (cg // 512) % 2
                    ck = (o * n_in + cg) // 1024
                    pbuf, pkey = (qbf, "qbf") if ck % 2 == 0 else (qbf2, "qbf2")
                    P.op("scalar", lambda e, cg=cg: e.activation(
                        out=sgn[:, 0:512], in_=x2f32[:, cg:cg + 512], func=AF.Sign),
                        rd=["x2f32"], wr=["sgn"])
                    P.op("vector", lambda e: e.tensor_scalar(
                        out=sgn[:, 0:512], in0=sgn[:, 0:512],
                        scalar1=cols[:, C_HALF:C_HALF + 1], op0=ALU.mult, scalar2=None),
                        rd=["sgn", "cols"], wr=["sgn"])
                    P.op("vector", lambda e, cg=cg: e.scalar_tensor_tensor(
                        out=qfp[:, 0:512], in0=x2f32[:, cg:cg + 512],
                        scalar=cols[:, c_rcp:c_rcp + 1], op0=ALU.mult,
                        in1=sgn[:, 0:512], op1=ALU.add),
                        rd=["x2f32", "cols", "sgn"], wr=["qfp"])
                    P.op("vector", lambda e: e.tensor_copy(
                        out=qi32[:, 0:512], in_=qfp[:, 0:512]), rd=["qfp"], wr=["qi32"])
                    P.op("vector", lambda e, half=half, pbuf=pbuf: e.tensor_copy(
                        out=pbuf[:, half * 512:half * 512 + 512], in_=qi32[:, 0:512]),
                        rd=["qi32"], wr=[pkey])
                    if half == 1:
                        P.mm_group([
                            (lambda e, kk=kk, pbuf=pbuf: e.transpose(
                                out=pt[:, kk * 128:(kk + 1) * 128],
                                in_=pbuf[:, kk * 128:(kk + 1) * 128],
                                identity=id128[:]))
                            for kk in range(8)], rd=[pkey, "id128"], wr=["pt"])
                        it2 = (o * n_in + cg) // 1024
                        ek, et = (("qbf2", qbf2) if it2 % 2 == 0 else ("sgn", sgn))
                        P.op("vector", lambda e, et=et: e.tensor_copy(
                            out=et[:, 0:1024], in_=pt[:]), rd=["pt"], wr=[ek])
                        pk = qkey + str(it2 % 2)
                        P.dma("gpsimd", lambda e, o=o, cg=cg, n_in=n_in, et=et: e.dma_start(
                            out=qnat[:, o * n_in + cg - 512:o * n_in + cg + 512],
                            in_=et[:, 0:1024]),
                            rd=[ek], wr=[pk], chan=pk, append=True)

        quant_weight(w["Wfc"], FF, D, wfc_s, "wfc_s", C_RCPF, C_SF, C_R2F)
        quant_weight(w["Wpj"], D, FF, wpj_s, "wpj_s", C_RCPP, C_SP, C_R2P)

        P.op("vector", lambda e: e.memset(fca[:], 0.0), wr=["fca"])
        P.op("vector", lambda e: e.memset(pja[:], 0.0), wr=["pja"])
        bfcv = w["Bfc"].rearrange("(a p) c -> p a c", p=128)
        P.dma("gpsimd", lambda e: e.dma_start(
            out=x2f32[:, 0:512].rearrange("p (a c) -> p a c", c=16), in_=bfcv),
            rd=["x2f32"], wr=["x2f32"], chan="qwl")
        P.op("vector", lambda e: e.tensor_copy(out=qbf[:, 0:512], in_=x2f32[:, 0:512]),
             rd=["x2f32"], wr=["qbf"])
        for ob in range(32):
            P.mm_group([lambda e, ob=ob: e.transpose(
                out=pt[0:16, 0:128], in_=qbf[:, ob * 16:(ob + 1) * 16],
                identity=id128[:])], rd=["qbf", "id128"], wr=["pt"])
            P.op("scalar", lambda e, ob=ob: e.activation(
                out=fca[0:16, ob * 128:(ob + 1) * 128], in_=pt[0:16, 0:128],
                func=AF.Identity, scale=cols[0:16, C_R2F:C_R2F + 1]),
                rd=["pt", "cols"], wr=["fca"])
        P.dma("gpsimd", lambda e: e.dma_start(out=x2f32[0:1, 0:4096], in_=w["bfc"][:, :]),
              rd=["x2f32"], wr=["x2f32"], chan="qwl")
        P.op("scalar", lambda e: e.activation(
            out=qbf[0:1, 0:1024][0:1, 0:1024] if False else qbf[0:1, 0:1024],
            in_=x2f32[0:1, 0:1024], func=AF.Identity,
            scale=cols[0:1, C_RCPF:C_RCPF + 1]), rd=["x2f32", "cols"], wr=["qbf"])
        P.dma("sync", lambda e: e.dma_start(
            out=fca[16:17, 0:1024], in_=qbf[0:1, 0:1024]),
            rd=["qbf"], wr=["fca"], chan="c_fca")
        for qq in range(1, 4):
            P.op("scalar", lambda e, qq=qq: e.activation(
                out=qbf[0:1, 0:1024], in_=x2f32[0:1, qq * 1024:(qq + 1) * 1024],
                func=AF.Identity, scale=cols[0:1, C_RCPF:C_RCPF + 1]),
                rd=["x2f32", "cols"], wr=["qbf"])
            P.dma("sync", lambda e, qq=qq: e.dma_start(
                out=fca[16:17, qq * 1024:(qq + 1) * 1024], in_=qbf[0:1, 0:1024]),
                rd=["qbf"], wr=["fca"], chan="c_fca")

        bpjv = w["Bpj"].rearrange("(a p) c -> p a c", p=128)
        P.dma("gpsimd", lambda e: e.dma_start(
            out=x2f32[:, 0:128].rearrange("p (a c) -> p a c", c=16), in_=bpjv),
            rd=["x2f32"], wr=["x2f32"], chan="qwl")
        P.op("vector", lambda e: e.tensor_copy(out=qbf[:, 0:128], in_=x2f32[:, 0:128]),
             rd=["x2f32"], wr=["qbf"])
        for ob in range(8):
            P.mm_group([lambda e, ob=ob: e.transpose(
                out=pt[0:16, 0:128], in_=qbf[:, ob * 16:(ob + 1) * 16],
                identity=id128[:])], rd=["qbf", "id128"], wr=["pt"])
            P.op("scalar", lambda e, ob=ob: e.activation(
                out=pja[0:16, ob * 128:(ob + 1) * 128], in_=pt[0:16, 0:128],
                func=AF.Identity, scale=cols[0:16, C_R2P:C_R2P + 1]),
                rd=["pt", "cols"], wr=["pja"])
        P.dma("gpsimd", lambda e: e.dma_start(out=x2f32[0:1, 0:1024], in_=w["bpj"][:, :]),
              rd=["x2f32"], wr=["x2f32"], chan="qwl")
        P.op("scalar", lambda e: e.activation(
            out=qbf[0:1, 0:1024], in_=x2f32[0:1, 0:1024], func=AF.Identity,
            scale=cols[0:1, C_RCPP:C_RCPP + 1]), rd=["x2f32", "cols"], wr=["qbf"])
        P.dma("sync", lambda e: e.dma_start(
            out=pja[16:17, 0:1024], in_=qbf[0:1, 0:1024]),
            rd=["qbf"], wr=["pja"], chan="c_pja")

        P.defer = False

        P.inherit(["vt0", "vt1", "vt2", "vt3", "gs0", "gs1", "gs2", "gs3"] + AR,
                  ["xT", "xtm", "sqs"])

        def arena_load(slot_ids, scratch, skey, cols_lo, ncols):
            keys = []
            for i, off in enumerate(range(0, ncols, 4096)):
                si = slot_ids[i]
                key = AR[si]
                keys.append(key)
                P.dma("sync", lambda e, si=si, off=off: e.dma_start(
                    out=ar[si], in_=scratch[:, cols_lo + off:cols_lo + off + 4096]),
                    rd=[skey], wr=[key], chan=key)
            return keys

        qproj_rot = ["pq0", "pq1"]

        for t in range(n_tiles):
            if STAGE < 2:
                break
            P.dma("gpsimd", lambda e, t=t: e.dma_start(
                out=xtm[:].rearrange("p (n c) -> p n c", c=1024),
                in_=xv[:, 4 * t:4 * t + 4, :]), wr=["xtm"], chan="xld")
            for k in range(8):
                bank = qproj_rot[k % 2]
                pb = halves[bank]
                P.mm_group([
                    (lambda e, j=j, k=k, pb=pb: e.transpose(
                        out=pb[:, j * 128:(j + 1) * 128],
                        in_=xtm[:, j * 1024 + k * 128:j * 1024 + (k + 1) * 128],
                        identity=idf32[:]))
                    for j in range(4)], rd=["xtm", "idf32"], wr=[bank])
                eng = "scalar" if k % 2 == 0 else "vector"
                if eng == "scalar":
                    P.op(eng, lambda e, k=k, pb=pb: e.activation(
                        out=xT[:, k * 512:(k + 1) * 512], in_=pb[:], func=AF.Copy,
                        bias=0.0), rd=[bank], wr=["xT"])
                else:
                    P.op(eng, lambda e, k=k, pb=pb: e.tensor_copy(
                        out=xT[:, k * 512:(k + 1) * 512], in_=pb[:]),
                        rd=[bank], wr=["xT"])

            if STAGE < 3:
                continue
            P.pump(PUMP)
            kq = arena_load((0, 1), wq_s, "wq_s", 0, 8192)
            for o in range(8):
                bank = qproj_rot[o % 2]
                pb = halves[bank]
                wkey = kq[0] if o < 4 else kq[1]
                wsl = ar[0] if o < 4 else ar[1]
                ems = []
                for kk in range(8):
                    c0 = (o % 4) * 1024 + kk * 128
                    ems.append(lambda e, pb=pb, wsl=wsl, c0=c0, kk=kk: e.matmul(
                        out=pb[:], lhsT=wsl[:, c0:c0 + 128],
                        rhs=xT[:, kk * 512:(kk + 1) * 512],
                        start=(kk == 0), stop=(kk == 7)))
                P.mm_group(ems, rd=[wkey, "xT"], wr=[bank])
                P.op("vector", lambda e, o=o, pb=pb: e.tensor_copy(
                    out=qf[:, o * 512:(o + 1) * 512], in_=pb[:]), rd=[bank], wr=["qfb"])
            kk_ = arena_load((2, 3), wk_s, "wk_s", 0, 8192)
            for o in range(8):
                bank = qproj_rot[o % 2]
                pb = halves[bank]
                wkey = kk_[0] if o < 4 else kk_[1]
                wsl = ar[2] if o < 4 else ar[3]
                ems = []
                for kk in range(8):
                    c0 = (o % 4) * 1024 + kk * 128
                    ems.append(lambda e, pb=pb, wsl=wsl, c0=c0, kk=kk: e.matmul(
                        out=pb[:], lhsT=wsl[:, c0:c0 + 128],
                        rhs=xT[:, kk * 512:(kk + 1) * 512],
                        start=(kk == 0), stop=(kk == 7)))
                P.mm_group(ems, rd=[wkey, "xT"], wr=[bank])
                P.op("vector", lambda e, o=o, pb=pb: e.tensor_copy(
                    out=kfm[:, o * 512:(o + 1) * 512], in_=pb[:]), rd=[bank], wr=["kfb"])

            if STAGE < 4:
                continue
            P.pump(PUMP)
            P.mm_group([
                (lambda e, kk=kk: e.matmul(
                    out=pu[:], lhsT=wgk1p[:, kk * 128:(kk + 1) * 128],
                    rhs=xT[:, kk * 512:(kk + 1) * 512],
                    start=(kk == 0), stop=(kk == 7)))
                for kk in range(8)], rd=["wgk1p", "xT"], wr=["pu"])
            P.op("vector", lambda e: e.tensor_copy(out=t1a[0:16, :], in_=pu[0:16, :]),
                 rd=["pu"], wr=["t1a"])

            P.pump(PUMP)
            kv = arena_load((0, 1, 2, 3), wv_s, "wv_s", 0, 16384)
            for j in range(4):
                for ohg in range(4):
                    wsl = ar[ohg]
                    wkey = kv[ohg]
                    bank = qproj_rot[(j * 4 + ohg) % 2]
                    pb = halves[bank]
                    ems = []
                    for kk in range(8):
                        rc = kk * 512
                        ems.append(lambda e, pb=pb, wsl=wsl, rc=rc, kk=kk, j=j: e.matmul(
                            out=pb[:], lhsT=xT[:, kk * 512 + j * 128:kk * 512 + (j + 1) * 128],
                            rhs=wsl[:, rc:rc + 512], start=(kk == 0), stop=(kk == 7)))
                    P.mm_group(ems, rd=[wkey, "xT"], wr=[bank])
                    P.op("vector", lambda e, j=j, ohg=ohg, pb=pb: e.tensor_copy(
                        out=vt[:, j * 2048 + ohg * 512:j * 2048 + (ohg + 1) * 512],
                        in_=pb[:]), rd=[bank], wr=["vt%d" % j])
            P.pump(PUMP)
            kg = arena_load((0, 1, 2, 3), wg_s, "wg_s", 0, 16384)
            for j in range(4):
                for ohg in range(4):
                    wsl = ar[ohg]
                    wkey = kg[ohg]
                    bank = qproj_rot[(j * 4 + ohg) % 2]
                    pb = halves[bank]
                    ems = []
                    for kk in range(8):
                        rc = kk * 512
                        ems.append(lambda e, pb=pb, wsl=wsl, rc=rc, kk=kk, j=j: e.matmul(
                            out=pb[:], lhsT=xT[:, kk * 512 + j * 128:kk * 512 + (j + 1) * 128],
                            rhs=wsl[:, rc:rc + 512], start=(kk == 0), stop=(kk == 7)))
                    P.mm_group(ems, rd=[wkey, "xT"], wr=[bank])
                    P.op("scalar", lambda e, j=j, ohg=ohg, pb=pb: e.activation(
                        out=gs[:, j * 2048 + ohg * 512:j * 2048 + (ohg + 1) * 512],
                        in_=pb[:], func=AF.Silu), rd=[bank], wr=["gs%d" % j])

            P.pump(PUMP)
            ko = arena_load((0, 1), wo_s, "wo_s", 0, 8192)
            ko += arena_load((2, 3), wo_s, "wo_s", 8192, 8192)

            if t % 4 == 0:
                P.op("vector", lambda e: e.memset(St[:], 0.0),
                     wr=["S%d" % i for i in range(8)])

            if STAGE < 5:
                continue
            pend = []
            for c in range(4):
                for hb in range(2):
                    gb = "pq0" if hb == 0 else "pq1"
                    gp = halves[gb]
                    sph = "sphl%d" % hb
                    P.mm_group([lambda e, gp=gp, hb=hb, c=c: e.matmul(
                        out=gp[:], lhsT=t1a[:, c * 128:(c + 1) * 128],
                        rhs=wgk2a[:, hb * 512:(hb + 1) * 512],
                        start=True, stop=True)], rd=["t1a", "wgk2a"], wr=[gb])
                    P.op("scalar", lambda e, gp=gp, hb=hb: e.activation(
                        out=spf[:, hb * 512:(hb + 1) * 512], in_=gp[:],
                        func=AF.Exp, scale=-1.0), rd=[gb], wr=["spf%d" % hb])
                    P.op("scalar", lambda e, hb=hb: e.activation(
                        out=spf[:, hb * 512:(hb + 1) * 512],
                        in_=spf[:, hb * 512:(hb + 1) * 512], func=AF.Ln, bias=1.0),
                        rd=["spf%d" % hb], wr=["spf%d" % hb])
                    P.op("vector", lambda e, hb=hb: e.tensor_copy(
                        out=sphl[:, hb * 512:(hb + 1) * 512],
                        in_=spf[:, hb * 512:(hb + 1) * 512]),
                        rd=["spf%d" % hb], wr=[sph])
                    P.op("vector", lambda e, hb=hb: e.tensor_sub(
                        sphl[:, 1024 + hb * 512:1024 + (hb + 1) * 512],
                        spf[:, hb * 512:(hb + 1) * 512],
                        sphl[:, hb * 512:(hb + 1) * 512]),
                        rd=["spf%d" % hb, sph], wr=[sph])
                    ems = []
                    for kk4 in range(4):
                        kk = hb * 4 + kk4
                        ems.append(lambda e, gp=gp, kk4=kk4, kk=kk: e.matmul(
                            out=gp[:, kk4 * 128:(kk4 + 1) * 128],
                            lhsT=sphl[:, kk * 128:(kk + 1) * 128], rhs=Ub[:],
                            start=True, stop=False))
                        ems.append(lambda e, gp=gp, kk4=kk4, kk=kk: e.matmul(
                            out=gp[:, kk4 * 128:(kk4 + 1) * 128],
                            lhsT=sphl[:, 1024 + kk * 128:1024 + (kk + 1) * 128], rhs=Ub[:],
                            start=False, stop=True))
                    P.mm_group(ems, rd=[sph, "Ub"], wr=[gb])
                    db = "pr0" if hb == 0 else "pr1"
                    dp = halves[db]
                    ems = []
                    for kk4 in range(4):
                        kk = hb * 4 + kk4
                        ems.append(lambda e, dp=dp, kk4=kk4, kk=kk: e.matmul(
                            out=dp[:, kk4:kk4 + 1],
                            lhsT=sphl[:, kk * 128:(kk + 1) * 128],
                            rhs=ones1[:], start=True, stop=False))
                        ems.append(lambda e, dp=dp, kk4=kk4, kk=kk: e.matmul(
                            out=dp[:, kk4:kk4 + 1],
                            lhsT=sphl[:, 1024 + kk * 128:1024 + (kk + 1) * 128],
                            rhs=ones1[:], start=False, stop=True))
                    P.mm_group(ems, rd=[sph, "ones1"], wr=[db])
                    P.op("scalar", lambda e, dp=dp, hb=hb: e.activation(
                        out=cols[:, C_DC + 4 * hb:C_DC + 4 * hb + 4], in_=dp[:, 0:4],
                        func=AF.Exp, scale=-1.0 / 16.0), rd=[db], wr=["cols"])
                    for d4 in range(4):
                        dd = hb * 4 + d4
                        P.op("vector", lambda e, dd=dd: e.tensor_scalar(
                            out=diagd[:, dd * 128:(dd + 1) * 128], in0=id128[:],
                            scalar1=cols[:, C_DC + dd:C_DC + dd + 1], op0=ALU.mult,
                            scalar2=None), rd=["id128", "cols"], wr=["diagd%d" % hb])
                    P.op("scalar", lambda e, gp=gp, hb=hb: e.activation(
                        out=eqc[:, hb * 512:(hb + 1) * 512], in_=gp[:], func=AF.Exp,
                        bias=cols[:, C_NLN16:C_NLN16 + 1]),
                        rd=[gb, "cols"], wr=["eqc%d" % hb])
                    P.op("scalar", lambda e, gp=gp, hb=hb: e.activation(
                        out=ekc[:, hb * 512:(hb + 1) * 512], in_=gp[:], func=AF.Exp,
                        scale=-1.0), rd=[gb], wr=["ekc%d" % hb])
                    P.mm_group([
                        (lambda e, gp=gp, hb=hb: e.matmul(
                            out=gp[:], lhsT=Lb[:], rhs=sphl[:, hb * 512:(hb + 1) * 512],
                            start=True, stop=False)),
                        (lambda e, gp=gp, hb=hb: e.matmul(
                            out=gp[:], lhsT=Lb[:],
                            rhs=sphl[:, 1024 + hb * 512:1024 + (hb + 1) * 512],
                            start=False, stop=True))], rd=[sph, "Lb"], wr=[gb])
                    P.op("scalar", lambda e, gp=gp, hb=hb: e.activation(
                        out=erc[:, hb * 512:(hb + 1) * 512], in_=gp[:], func=AF.Exp),
                        rd=[gb], wr=["erc%d" % hb])
                    qv3 = qf[:].rearrange("p (o v) -> p o v", v=512)
                    kv3 = kfm[:].rearrange("p (o v) -> p o v", v=512)
                    e3 = lambda buf, hb: buf[:].rearrange(
                        "p (o v) -> p o v", v=128)[:, 4 * hb:4 * hb + 4, :]
                    P.op("vector", lambda e, c=c, hb=hb: e.tensor_tensor(
                        out=e3(qg, hb), in0=qv3[:, 4 * hb:4 * hb + 4, c * 128:(c + 1) * 128],
                        in1=e3(eqc, hb), op=ALU.mult),
                        rd=["qfb", "eqc%d" % hb], wr=["qg%d" % hb])
                    P.op("vector", lambda e, c=c, hb=hb: e.tensor_tensor(
                        out=e3(ki, hb), in0=kv3[:, 4 * hb:4 * hb + 4, c * 128:(c + 1) * 128],
                        in1=e3(ekc, hb), op=ALU.mult),
                        rd=["kfb", "ekc%d" % hb], wr=["ki%d" % hb])
                    P.mm_group([
                        (lambda e, kk=kk, c=c: e.transpose(
                            out=pt[:, kk * 128:(kk + 1) * 128],
                            in_=kfm[:, kk * 512 + c * 128:kk * 512 + (c + 1) * 128],
                            identity=id128[:]))
                        for kk in range(4 * hb, 4 * hb + 4)], rd=["kfb", "id128"], wr=["pt"])
                    P.op("vector", lambda e, hb=hb: e.tensor_copy(
                        out=kt[:, hb * 512:(hb + 1) * 512],
                        in_=pt[:, hb * 512:(hb + 1) * 512]), rd=["pt"], wr=["kt%d" % hb])
                    P.op("vector", lambda e, hb=hb: e.tensor_mul(
                        kt[:, hb * 512:(hb + 1) * 512], kt[:, hb * 512:(hb + 1) * 512],
                        erc[:, hb * 512:(hb + 1) * 512]),
                        rd=["kt%d" % hb, "erc%d" % hb], wr=["kt%d" % hb])
                    for h in (2 * hb, 2 * hb + 1):
                        P.mm_group([
                            (lambda e, h=h, d=d: e.matmul(
                                out=pu[:, h * 128:(h + 1) * 128],
                                lhsT=ki[:, (2 * h + d) * 128:(2 * h + d + 1) * 128],
                                rhs=qg[:, (2 * h + d) * 128:(2 * h + d + 1) * 128],
                                start=(d == 0), stop=(d == 1)))
                            for d in range(2)],
                            rd=["ki%d" % hb, "qg%d" % hb], wr=["pu%d" % hb])
                        P.op("vector", lambda e, h=h: e.tensor_mul(
                            scm[:, h * 128:(h + 1) * 128], pu[:, h * 128:(h + 1) * 128],
                            mskT[:]), rd=["pu%d" % hb, "mskT"], wr=["scm%d" % hb])
                    for h in (2 * hb, 2 * hb + 1):
                        bank = "po0" if h % 2 == 0 else "po1"
                        pb = halves[bank]
                        ems = [(lambda e, h=h, c=c, pb=pb: e.matmul(
                            out=pb[:], lhsT=scm[:, h * 128:(h + 1) * 128],
                            rhs=vt[:, c * 2048 + h * 512:c * 2048 + (h + 1) * 512],
                            start=True, stop=False))]
                        for d in range(2):
                            ems.append(lambda e, h=h, d=d, pb=pb: e.matmul(
                                out=pb[:], lhsT=qg[:, (2 * h + d) * 128:(2 * h + d + 1) * 128],
                                rhs=St[:, (2 * h + d) * 512:(2 * h + d + 1) * 512],
                                start=False, stop=(d == 1)))
                        P.mm_group(ems, rd=["scm%d" % hb, "qg%d" % hb, "vt%d" % c,
                                            "S%d" % (2 * h), "S%d" % (2 * h + 1)],
                                   wr=[bank])
                        P.op("scalar", lambda e, pb=pb, h=h: e.activation(
                            out=sqs[:, 0:512], in_=pb[:], func=AF.Square,
                            accum_out=cols[:, C_SSQ + h:C_SSQ + h + 1]),
                            rd=[bank], wr=["sqs", "cols"])
                    P.op("scalar", lambda e, hb=hb: e.activation(
                        out=cols[:, C_R + 2 * hb:C_R + 2 * hb + 2],
                        in_=cols[:, C_SSQ + 2 * hb:C_SSQ + 2 * hb + 2],
                        func=AF.Ln, scale=1.0 / 512.0, bias=cols[:, C_EPS:C_EPS + 1]),
                        rd=["cols"], wr=["cols"])
                    P.op("scalar", lambda e, hb=hb: e.activation(
                        out=cols[:, C_R + 2 * hb:C_R + 2 * hb + 2],
                        in_=cols[:, C_R + 2 * hb:C_R + 2 * hb + 2],
                        func=AF.Exp, scale=-0.5), rd=["cols"], wr=["cols"])
                    for h in (2 * hb, 2 * hb + 1):
                        bank = "po0" if h % 2 == 0 else "po1"
                        pb = halves[bank]
                        P.op("vector", lambda e, pb=pb, h=h, c=c: e.scalar_tensor_tensor(
                            out=og[:, h * 512:(h + 1) * 512], in0=pb[:],
                            scalar=cols[:, C_R + h:C_R + h + 1], op0=ALU.mult,
                            in1=gs[:, c * 2048 + h * 512:c * 2048 + (h + 1) * 512],
                            op1=ALU.mult), rd=[bank, "cols", "gs%d" % c],
                            wr=["og%d" % hb])
                    for h in (2 * hb, 2 * hb + 1):
                        for d in range(2):
                            sb = "po0" if (h * 2 + d) % 2 == 0 else "po1"
                            pb = halves[sb]
                            sk = "S%d" % (2 * h + d)
                            dd = 2 * h + d
                            P.mm_group([
                                (lambda e, pb=pb, dd=dd: e.matmul(
                                    out=pb[:], lhsT=diagd[:, dd * 128:(dd + 1) * 128],
                                    rhs=St[:, dd * 512:(dd + 1) * 512],
                                    start=True, stop=False)),
                                (lambda e, pb=pb, h=h, d=d, c=c: e.matmul(
                                    out=pb[:], lhsT=kt[:, (2 * h + d) * 128:(2 * h + d + 1) * 128],
                                    rhs=vt[:, c * 2048 + h * 512:c * 2048 + (h + 1) * 512],
                                    start=False, stop=True))],
                                rd=["diagd%d" % hb, "kt%d" % hb, "vt%d" % c, sk],
                                wr=[sb])
                            P.op("vector", lambda e, pb=pb, dd=dd: e.tensor_copy(
                                out=St[:, dd * 512:(dd + 1) * 512],
                                in_=pb[:]), rd=[sb], wr=[sk])
                    g2 = hb
                    P.mm_group([
                        (lambda e, b=b, g2=g2: e.transpose(
                            out=pt[:, b * 128:(b + 1) * 128],
                            in_=og[:, g2 * 1024 + b * 128:g2 * 1024 + (b + 1) * 128],
                            identity=id128[:]))
                        for b in range(8)], rd=["og%d" % hb, "id128"], wr=["pt"])
                    for j in range(4):
                        pv = pt[:].rearrange("p (b c) -> p b c", c=128)[:, j::4, :]
                        xv2 = xT[:, g2 * 1024:(g2 + 1) * 1024].rearrange(
                            "p (b c) -> p b c", c=128)[:, j::4, :]
                        P.op("vector", lambda e, pv=pv, xv2=xv2, j=j: e.tensor_scalar(
                            out=xv2, in0=pv, scalar1=cols[:, C_GNW + j:C_GNW + j + 1],
                            op0=ALU.mult, scalar2=None),
                            rd=["pt", "cols"], wr=["xT"])
                    if hb == 0:
                        while pend:
                            pend.pop(0)()
                for oh in range(2):
                    bank = "pr0" if oh == 0 else "pr1"
                    pb = halves[bank]
                    ems = []
                    for dvk in range(16):
                        wsl = ar[(oh * 2) + (dvk // 8)]
                        rc = (dvk % 8) * 512
                        ems.append(lambda e, pb=pb, wsl=wsl, rc=rc, dvk=dvk: e.matmul(
                            out=pb[:], lhsT=xT[:, dvk * 128:(dvk + 1) * 128],
                            rhs=wsl[:, rc:rc + 512], start=(dvk == 0), stop=(dvk == 15)))
                    P.mm_group(ems, rd=["xT"] + ko, wr=[bank])
                P.op("vector", lambda e, c=c: e.scalar_tensor_tensor(
                    out=x1r[:], in0=pr[:], scalar=1.0, op0=ALU.mult,
                    in1=xtm[:, c * 1024:(c + 1) * 1024], op1=ALU.add,
                    accum_out=cols[:, C_LN:C_LN + 1]),
                    rd=["pr0", "pr1", "xtm", ], wr=["x1r", "cols"])
                P.op("scalar", lambda e: e.activation(
                    out=sqs[:, 0:1024], in_=x1r[:], func=AF.Square,
                    accum_out=cols[:, C_LN + 1:C_LN + 2]),
                    rd=["x1r"], wr=["sqs", "cols"])
                P.op("vector", lambda e: e.tensor_scalar(
                    out=cols[:, C_LN + 2:C_LN + 3], in0=cols[:, C_LN:C_LN + 1],
                    scalar1=1.0 / 1024.0, op0=ALU.mult, scalar2=None),
                    rd=["cols"], wr=["cols"])
                P.op("vector", lambda e: e.tensor_mul(
                    cols[:, C_LN + 3:C_LN + 4], cols[:, C_LN + 2:C_LN + 3],
                    cols[:, C_LN + 2:C_LN + 3]), rd=["cols"], wr=["cols"])
                P.op("vector", lambda e: e.tensor_scalar(
                    out=cols[:, C_LN + 4:C_LN + 5], in0=cols[:, C_LN + 1:C_LN + 2],
                    scalar1=1.0 / 1024.0, op0=ALU.mult,
                    scalar2=cols[:, C_LN + 3:C_LN + 4], op1=ALU.subtract),
                    rd=["cols"], wr=["cols"])
                P.op("scalar", lambda e: e.activation(
                    out=cols[:, C_LN + 5:C_LN + 6], in_=cols[:, C_LN + 4:C_LN + 5],
                    func=AF.Ln, bias=cols[:, C_EPS:C_EPS + 1]), rd=["cols"], wr=["cols"])
                P.op("scalar", lambda e: e.activation(
                    out=cols[:, C_LN + 5:C_LN + 6], in_=cols[:, C_LN + 5:C_LN + 6],
                    func=AF.Exp, scale=-0.5), rd=["cols"], wr=["cols"])
                P.op("vector", lambda e: e.tensor_scalar(
                    out=x1c[:], in0=x1r[:], scalar1=cols[:, C_LN + 2:C_LN + 3],
                    op0=ALU.subtract, scalar2=cols[:, C_LN + 5:C_LN + 6],
                    op1=ALU.mult), rd=["x1r", "cols"], wr=["x1c"])
                xbslice = x1bf[:, (c % 2) * 1024:(c % 2) * 1024 + 1024]
                P.op("vector", lambda e, xbslice=xbslice: e.tensor_scalar(
                    out=xbslice, in0=x1r[:], scalar1=cols[:, C_LN + 2:C_LN + 3],
                    op0=ALU.subtract, scalar2=cols[:, C_LN + 5:C_LN + 6],
                    op1=ALU.mult), rd=["x1r", "cols"], wr=["x1bf%d" % (c % 2)])
                P.dma("gpsimd", lambda e, t=t, c=c: e.dma_start(
                    out=x1v[:, 4 * t + c, :], in_=x1c[:]), rd=["x1c"], wr=["x1_s"],
                    chan="x1st", append=True)
                def _x1T(c=c, xbslice=xbslice):
                    P.mm_group([
                        (lambda e, kk=kk: e.transpose(
                            out=pt[:, kk * 128:(kk + 1) * 128],
                            in_=xbslice[:, kk * 128:(kk + 1) * 128], identity=id128[:]))
                        for kk in range(8)], rd=["x1bf%d" % (c % 2), "id128"], wr=["pt"])
                    xts = x1Tt[:, (c % 2) * 1024:(c % 2) * 1024 + 1024]
                    P.op("vector", lambda e, xts=xts: e.tensor_copy(
                        out=xts, in_=pt[:]), rd=["pt"], wr=["x1Tt%d" % (c % 2)])
                    P.dma("gpsimd", lambda e, t=t, c=c, xts=xts: e.dma_start(
                        out=x1T_s[:, t * 4096:(t + 1) * 4096].rearrange(
                            "p (k v) -> p k v", v=512)[:, :, c * 128:(c + 1) * 128],
                        in_=xts.rearrange("p (k v) -> p k v", v=128)),
                        rd=["x1Tt%d" % (c % 2)], wr=["x1T_s%d" % (c % 2)],
                        chan="x1Tst%d" % (c % 2), append=True)
                pend.append(_x1T)
            while pend:
                pend.pop(0)()

        P.flush_bg()

        if not skip_b:
            P.inherit(H1K + AR, ["vt%d" % j for j in range(4)] +
                      ["gs%d" % j for j in range(4)] + AR + ["xT"])
            for t in range(n_b_tiles):
                P.dma("gpsimd", lambda e, t=t: e.dma_start(
                    out=qf[:], in_=x1T_s[:, t * 4096:(t + 1) * 4096]),
                    rd=["x1T_s0", "x1T_s1"], wr=["qfb"], chan="x1Tld")
                P.dma("gpsimd", lambda e, t=t: e.dma_start(
                    out=xtm[:].rearrange("p (n c) -> p n c", c=1024),
                    in_=x1v[:, 4 * t:4 * t + 4, :]), rd=["x1_s"], wr=["xtm"],
                    chan="x1ld")
                P.mm_group([
                    (lambda e, kk=kk: e.matmul(
                        out=pu[0:16, :], lhsT=afcT[:, kk * 16:(kk + 1) * 16],
                        rhs=qf[:, kk * 512:(kk + 1) * 512],
                        start=(kk == 0), stop=(kk == 7)))
                    for kk in range(8)], rd=["afcT", "qfb"], wr=["pu"])
                P.op("vector", lambda e: e.tensor_copy(out=za[0:16, :], in_=pu[0:16, :]),
                     rd=["pu"], wr=["za"])
                for oc in range(8):
                    slot = ar[oc % 4]
                    skey = AR[oc % 4]
                    P.dma("sync", lambda e, oc=oc, slot=slot: e.dma_start(
                        out=slot, in_=wfc_s[:, oc * 4096:(oc + 1) * 4096]),
                        rd=["wfc_s0", "wfc_s1"], wr=[skey], chan=skey)
                    for o4 in range(4):
                        o = oc * 4 + o4
                        bank = qproj_rot[o % 2]
                        pb = halves[bank]
                        ems = []
                        for kk in range(8):
                            ems.append(lambda e, pb=pb, slot=slot, o4=o4, kk=kk: e.matmul(
                                out=pb[:], lhsT=slot[:, o4 * 1024 + kk * 128:o4 * 1024 + (kk + 1) * 128],
                                rhs=qf[:, kk * 512:(kk + 1) * 512],
                                start=(kk == 0), stop=False))
                        ems.append(lambda e, pb=pb, o=o: e.matmul(
                            out=pb[:], lhsT=fca[:, o * 128:(o + 1) * 128],
                            rhs=za[:], start=False, stop=True))
                        P.mm_group(ems, rd=[skey, "qfb", "fca", "za"], wr=[bank])
                        P.op("scalar", lambda e, pb=pb, o=o: e.activation(
                            out=h1[:, o * 512:(o + 1) * 512], in_=pb[:], func=AF.Gelu,
                            scale=cols[:, C_SF:C_SF + 1]),
                            rd=[bank, "cols"], wr=["h1_%d" % o])
                P.mm_group([
                    (lambda e, kk=kk: e.matmul(
                        out=pu[0:16, :], lhsT=apjT[:, kk * 16:(kk + 1) * 16],
                        rhs=h1[:, kk * 512:(kk + 1) * 512],
                        start=(kk == 0), stop=(kk == 31)))
                    for kk in range(32)], rd=H1K + ["apjT"], wr=["pu"])
                P.op("vector", lambda e: e.tensor_copy(out=z2a[0:16, :], in_=pu[0:16, :]),
                     rd=["pu"], wr=["z2a"])
                for o in range(8):
                    slot = ar[o % 4]
                    skey = AR[o % 4]
                    P.dma("sync", lambda e, o=o, slot=slot: e.dma_start(
                        out=slot, in_=wpj_s[:, o * 4096:(o + 1) * 4096]),
                        rd=["wpj_s0", "wpj_s1"], wr=[skey], chan=skey)
                    bank = ["po0", "po1"][o % 2]
                    pb = halves[bank]
                    ems = []
                    for kk in range(32):
                        ems.append(lambda e, pb=pb, slot=slot, kk=kk: e.matmul(
                            out=pb[:], lhsT=slot[:, kk * 128:(kk + 1) * 128],
                            rhs=h1[:, kk * 512:(kk + 1) * 512],
                            start=(kk == 0), stop=False))
                    ems.append(lambda e, pb=pb, o=o: e.matmul(
                        out=pb[:], lhsT=pja[:, o * 128:(o + 1) * 128], rhs=z2a[:],
                        start=False, stop=True))
                    P.mm_group(ems, rd=[skey, "z2a", "pja"] + H1K, wr=[bank])
                    P.op("vector", lambda e, pb=pb, o=o: e.tensor_copy(
                        out=x2f32[:, o * 512:(o + 1) * 512], in_=pb[:]),
                        rd=[bank], wr=["x2f32"])
                for j in range(4):
                    for hb in range(2):
                        bank = "pq0" if hb == 0 else "pq1"
                        pb = halves[bank]
                        P.mm_group([
                            (lambda e, pb=pb, o4=o4, hb=hb, j=j: e.transpose(
                                out=pb[:, o4 * 128:(o4 + 1) * 128],
                                in_=x2f32[:, (hb * 4 + o4) * 512 + j * 128:
                                          (hb * 4 + o4) * 512 + (j + 1) * 128],
                                identity=idf32[:]))
                            for o4 in range(4)], rd=["x2f32", "idf32"], wr=[bank])
                    P.op("vector", lambda e, j=j: e.scalar_tensor_tensor(
                        out=x1r[:], in0=pq[:], scalar=cols[:, C_SP:C_SP + 1],
                        op0=ALU.mult, in1=xtm[:, j * 1024:(j + 1) * 1024], op1=ALU.add,
                        accum_out=cols[:, C_LN:C_LN + 1]),
                        rd=["pq0", "pq1", "xtm", "cols"], wr=["x1r", "cols"])
                    P.op("scalar", lambda e: e.activation(
                        out=sqs[:, 0:1024], in_=x1r[:], func=AF.Square,
                        accum_out=cols[:, C_LN + 1:C_LN + 2]),
                        rd=["x1r"], wr=["sqs", "cols"])
                    P.op("vector", lambda e: e.tensor_scalar(
                        out=cols[:, C_LN + 2:C_LN + 3], in0=cols[:, C_LN:C_LN + 1],
                        scalar1=1.0 / 1024.0, op0=ALU.mult, scalar2=None),
                        rd=["cols"], wr=["cols"])
                    P.op("vector", lambda e: e.tensor_mul(
                        cols[:, C_LN + 3:C_LN + 4], cols[:, C_LN + 2:C_LN + 3],
                        cols[:, C_LN + 2:C_LN + 3]), rd=["cols"], wr=["cols"])
                    P.op("vector", lambda e: e.tensor_scalar(
                        out=cols[:, C_LN + 4:C_LN + 5], in0=cols[:, C_LN + 1:C_LN + 2],
                        scalar1=1.0 / 1024.0, op0=ALU.mult,
                        scalar2=cols[:, C_LN + 3:C_LN + 4], op1=ALU.subtract),
                        rd=["cols"], wr=["cols"])
                    P.op("scalar", lambda e: e.activation(
                        out=cols[:, C_LN + 5:C_LN + 6], in_=cols[:, C_LN + 4:C_LN + 5],
                        func=AF.Ln, bias=cols[:, C_EPS:C_EPS + 1]), rd=["cols"], wr=["cols"])
                    P.op("scalar", lambda e: e.activation(
                        out=cols[:, C_LN + 5:C_LN + 6], in_=cols[:, C_LN + 5:C_LN + 6],
                        func=AF.Exp, scale=-0.5), rd=["cols"], wr=["cols"])
                    P.op("vector", lambda e: e.tensor_scalar(
                        out=x1c[:], in0=x1r[:], scalar1=cols[:, C_LN + 2:C_LN + 3],
                        op0=ALU.subtract, scalar2=cols[:, C_LN + 5:C_LN + 6],
                        op1=ALU.mult), rd=["x1r", "cols"], wr=["x1c"])
                    P.dma("gpsimd", lambda e, t=t, j=j: e.dma_start(
                        out=yv[:, 4 * t + j, :], in_=x1c[:]), rd=["x1c"], wr=["y_h"],
                        chan="yst", append=True)

        for ename in ENGINES:
            P.barrier(ename)

        import os as _os
        if _os.environ.get("PROBE_SBUF"):
            lo, hi = 0, 131072
            pidx = 0
            while lo < hi:
                mid = (lo + hi + 1) // 2
                pidx += 1
                try:
                    with nc.sbuf_tensor("probe%d" % pidx, [128, mid], mybir.dt.int8):
                        lo = mid
                except AssertionError:
                    hi = mid - 1
            print("SBUF headroom bytes/partition:", lo)
        block = es.enter_context(nc.Block())
        P.emit(nc, es, block)

    return nc


def host_constants():
    import ml_dtypes
    bf = ml_dtypes.bfloat16
    kb = np.zeros((128, 512), dtype=bf)
    kb[:, 0:128] = np.eye(128, dtype=np.float32).astype(bf)
    jj, ii = np.meshgrid(np.arange(128), np.arange(128), indexing="ij")
    kb[:, 128:256] = np.where(jj <= ii, np.float32(-1.0 / 16.0), 0.0).astype(bf)
    kb[:, 256:384] = np.where(jj > ii, np.float32(-1.0 / 16.0), 0.0).astype(bf)
    kb[:, 384:512] = np.where(jj <= ii, np.float32(1.0), 0.0).astype(bf)
    kfc = np.zeros((128, 8), dtype=np.float32)
    kfc[:, 0] = 1.5
    kfc[:, 1] = NEGLN16
    kfc[:, 2] = EPS
    return kb, kfc


def kernel(**inputs):
    kb, kfc = host_constants()
    nc = build_program()
    names = {
        "Wq": "L0_attn_Wq", "Wk": "L0_attn_Wk", "Wv": "L0_attn_Wv",
        "Wg": "L0_attn_Wg", "Wgk1": "L0_attn_Wgk1", "Wgk2": "L0_attn_Wgk2",
        "bgk2": "L0_attn_bgk2", "gnw": "L0_attn_g_norm_w", "Wo": "L0_attn_Wo",
        "Wfc": "L0_ffn_W_fc", "bfc": "L0_ffn_b_fc", "Afc": "L0_ffn_A_fc",
        "Bfc": "L0_ffn_B_fc", "Wpj": "L0_ffn_W_proj", "bpj": "L0_ffn_b_proj",
        "Apj": "L0_ffn_A_proj", "Bpj": "L0_ffn_B_proj",
    }
    x = np.ascontiguousarray(inputs["L0_attn_hidden_states"])
    in_maps = []
    for core in range(8):
        m = {"xin": x[2 * core:2 * core + 2].reshape(NTOK, D), "kb": kb, "kfc": kfc}
        for k, nm in names.items():
            a = inputs[nm]
            if a.ndim == 1:
                a = a.reshape(1, -1)
            m[k] = np.ascontiguousarray(a)
        in_maps.append(m)
    res = run_bass_kernel_spmd(nc, in_maps, core_ids=list(range(8)))
    out = np.concatenate([res.results[c]["y"].reshape(2, 2048, D) for c in range(8)],
                         axis=0)
    return out.astype(np.float32)


# version 15
# speedup vs baseline: 1.0257x; 1.0257x over previous
import contextlib
import math
import numpy as np
import concourse.bass as bass
import concourse.mybir as mybir
from concourse.bass_utils import run_bass_kernel_spmd

F32 = mybir.dt.float32
BF16 = mybir.dt.bfloat16
I32 = mybir.dt.int32
AF = mybir.ActivationFunctionType
ALU = mybir.AluOpType
AX = mybir.AxisListType

D = 1024
DK = 1024
DV = 2048
NH = 4
FF = 4096
LR = 16
T = 512
NTILES = 8
NTOK = NTILES * T
NEGLN16 = -math.log(16.0)
EPS = 1e-5

ENGINES = ("sync", "scalar", "vector", "tensor", "gpsimd")
STAGE = 99.0
PUMP = 32
BGN = 10**9


class Plan:

    def __init__(self):
        self.ops = {e: [] for e in ENGINES}
        self.counts = {}
        self.watermark = {}
        self.writer = {}
        self.readers = {}
        self.sem_names = []
        self.defer = False
        self.bgq = []

    def _sem(self, name):
        if name not in self.counts:
            self.counts[name] = 0
            self.sem_names.append(name)
        return name

    def _edges(self, ename, rd, wr, skip_war=False):
        need = {}
        def add(tok):
            s, v = tok
            if v > need.get(s, 0):
                need[s] = v
        for b in rd:
            t = self.writer.get(b)
            if t:
                add(t)
        for b in wr:
            if not skip_war:
                t = self.writer.get(b)
                if t:
                    add(t)
                for rt in self.readers.get(b, ()):
                    add(rt)
        waits = []
        for s, v in need.items():
            if self.watermark.get((ename, s), 0) < v:
                self.watermark[(ename, s)] = v
                waits.append((s, v))
        return waits

    def _post(self, tok, rd, wr):
        for b in wr:
            self.writer[b] = tok
            self.readers[b] = []
        for b in rd:
            self.readers.setdefault(b, []).append(tok)

    def op(self, ename, emit, rd=(), wr=()):
        if self.defer:
            self.bgq.append(("op", (ename, emit, rd, wr)))
            return
        waits = self._edges(ename, rd, wr)
        s = self._sem("e_" + ename)
        self.counts[s] += 1
        self.ops[ename].append((waits, emit, (s, 1)))
        self._post((s, self.counts[s]), rd, wr)

    def mm_group(self, emits, rd=(), wr=()):
        if self.defer:
            self.bgq.append(("mm", (emits, rd, wr)))
            return
        waits = self._edges("tensor", rd, wr)
        s = self._sem("e_tensor")
        self.counts[s] += 1
        n = len(emits)
        for i, em in enumerate(emits):
            self.ops["tensor"].append(
                (waits if i == 0 else [], em, (s, 1) if i == n - 1 else None))
        self._post((s, self.counts[s]), rd, wr)

    def dma(self, ename, emit, rd=(), wr=(), chan=None, append=False):
        if self.defer:
            self.bgq.append(("dma", (ename, emit, rd, wr, chan, append)))
            return
        waits = self._edges(ename, rd, wr, skip_war=append)
        chan = self._sem("d_" + (chan or (wr[0] if wr else rd[0])))
        self.counts[chan] += 16
        self.ops[ename].append((waits, emit, (chan, 16)))
        self._post((chan, self.counts[chan]), rd, wr)

    def pump(self, n):
        assert not self.defer
        if not hasattr(self, "npumped"):
            self.npumped = 0
        n = min(n, BGN - self.npumped)
        if n <= 0:
            return
        self.npumped += n
        for _ in range(min(n, len(self.bgq))):
            kind, args = self.bgq.pop(0)
            if kind == "op":
                self.op(*args)
            elif kind == "mm":
                self.mm_group(*args)
            else:
                ename, emit, rd, wr, chan, append = args
                self.dma(ename, emit, rd=rd, wr=wr, chan=chan, append=append)

    def flush_bg(self):
        while self.bgq:
            kind, args = self.bgq.pop(0)
            if kind == "op":
                self.op(*args)
            elif kind == "mm":
                self.mm_group(*args)
            else:
                ename, emit, rd, wr, chan, append = args
                self.dma(ename, emit, rd=rd, wr=wr, chan=chan, append=append)

    def inherit(self, dst_bufs, src_bufs):
        toks = []
        for b in src_bufs:
            if b in self.writer:
                toks.append(self.writer[b])
            toks.extend(self.readers.get(b, ()))
        for b in dst_bufs:
            for t in toks:
                self.readers.setdefault(b, []).append(t)

    def barrier(self, ename):
        waits = []
        for s in self.sem_names:
            v = self.counts[s]
            if v > 0 and self.watermark.get((ename, s), 0) < v:
                self.watermark[(ename, s)] = v
                waits.append((s, v))
        self.ops[ename].append((waits, None, None))

    def emit(self, nc, es, block):
        sems = {s: es.enter_context(nc.semaphore(s)) for s in self.sem_names}
        for ename in ENGINES:
            oplist = self.ops[ename]
            if not oplist:
                continue
            def body(eng, oplist=oplist):
                for waits, emitfn, inc in oplist:
                    for s, v in waits:
                        eng.wait_ge(sems[s], v)
                    if emitfn is not None:
                        inst = emitfn(eng)
                        if inc is not None:
                            inst.then_inc(sems[inc[0]], inc[1])
            getattr(block, ename)(body)


def build_program(n_tiles=NTILES, n_b_tiles=None, skip_b=False):
    if n_b_tiles is None:
        n_b_tiles = n_tiles
    nc = bass.Bass("TRN2", target_bir_lowering=False)
    P = Plan()

    xin = nc.dram_tensor("xin", [NTOK, D], F32, kind="ExternalInput").ap()
    w = {}
    for name, shape in [
        ("Wq", [D, DK]), ("Wk", [D, DK]), ("Wv", [D, DV]), ("Wg", [D, DV]),
        ("Wgk1", [D, LR]), ("Wgk2", [LR, DK]), ("bgk2", [1, DK]),
        ("gnw", [1, 512]), ("Wo", [DV, D]),
        ("Wfc", [FF, D]), ("bfc", [1, FF]), ("Afc", [LR, D]), ("Bfc", [FF, LR]),
        ("Wpj", [D, FF]), ("bpj", [1, D]), ("Apj", [LR, FF]), ("Bpj", [D, LR]),
    ]:
        w[name] = nc.dram_tensor(name, shape, F32, kind="ExternalInput").ap()
    kb = nc.dram_tensor("kb", [128, 512], BF16, kind="ExternalInput").ap()
    kfc = nc.dram_tensor("kfc", [128, 8], F32, kind="ExternalInput").ap()
    yout = nc.dram_tensor("y", [NTOK, D], F32, kind="ExternalOutput").ap()

    wq_s = nc.dram_tensor("wq_s", [128, 8 * 1024], BF16, kind="Internal").ap()
    wk_s = nc.dram_tensor("wk_s", [128, 8 * 1024], BF16, kind="Internal").ap()
    wv_s = nc.dram_tensor("wv_s", [128, 16384], BF16, kind="Internal").ap()
    wg_s = nc.dram_tensor("wg_s", [128, 16384], BF16, kind="Internal").ap()
    wo_s = nc.dram_tensor("wo_s", [128, 16384], BF16, kind="Internal").ap()
    wfc_s = nc.dram_tensor("wfc_s", [128, 32768], BF16, kind="Internal").ap()
    wpj_s = nc.dram_tensor("wpj_s", [128, 32768], BF16, kind="Internal").ap()
    wfcq_s = nc.dram_tensor("wfcq_s", [128, 32768], BF16, kind="Internal").ap()
    wpjq_s = nc.dram_tensor("wpjq_s", [128, 32768], BF16, kind="Internal").ap()
    x1_s = nc.dram_tensor("x1_s", [NTOK, D], F32, kind="Internal").ap()
    x1T_s = nc.dram_tensor("x1T_s", [128, 8 * 4096], BF16, kind="Internal").ap()

    xv = xin.rearrange("(n p) c -> p n c", p=128)
    yv = yout.rearrange("(n p) c -> p n c", p=128)
    x1v = x1_s.rearrange("(n p) c -> p n c", p=128)

    es = contextlib.ExitStack()
    with es:
        S_ = lambda name, shape, dt=BF16: es.enter_context(nc.sbuf_tensor(name, shape, dt))
        xtm = S_("xtm", [128, 4096], F32)
        x1r = S_("x1r", [128, 1024], F32)
        x1c = S_("x1c", [128, 1024], F32)
        spf = S_("spf", [128, 1024], F32)
        x2f32 = S_("x2f32", [128, 4096], F32)
        cols = S_("cols", [128, 64], F32)
        stats = S_("stats", [128, 32], F32)
        sqs = S_("sqs", [128, 1024])
        sgn = S_("sgn", [128, 1024])
        qi32 = S_("qi32", [128, 512], I32)
        qfp = S_("qfp", [128, 512], F32)
        qbf = S_("qbf", [128, 1024])
        qbf2 = S_("qbf2", [128, 1024])
        xT = S_("xT", [128, 4096])
        qf = S_("qf", [128, 4096])
        kfm = S_("kfm", [128, 4096])
        meg = S_("meg", [128, 32768])
        sphl = S_("sphl", [128, 2048])
        eqc = S_("eqc", [128, 1024])
        ekc = S_("ekc", [128, 1024])
        erc = S_("erc", [128, 1024])
        qg = S_("qg", [128, 1024])
        ki = S_("ki", [128, 1024])
        kt = S_("kt", [128, 1024])
        St = S_("St", [128, 4096])
        scm = S_("scm", [128, 512])
        og = S_("og", [128, 2048])
        x1bf = S_("x1bf", [128, 2048])
        x1Tt = S_("x1Tt", [128, 4096])
        diagd = S_("diagd", [128, 1024])
        wgk1p = S_("wgk1p", [128, 1024])
        wgk2a = S_("wgk2a", [128, 1024])
        afcT = S_("afcT", [128, 128])
        apjT = S_("apjT", [128, 512])
        fca = S_("fca", [128, 4096])
        pja = S_("pja", [128, 1024])
        t1a = S_("t1a", [128, 512])
        za = S_("za", [128, 512])
        z2a = S_("z2a", [128, 512])
        id128 = S_("id128", [128, 128])
        idf32 = S_("idf32", [128, 128], F32)
        ones1 = S_("ones1", [128, 1])
        Ub = S_("Ubm", [128, 128])
        Lb = S_("Lbm", [128, 128])
        mskT = S_("mskT", [128, 128])

        pq = es.enter_context(nc.psum_tensor("pq", [128, 1024], F32))
        pr = es.enter_context(nc.psum_tensor("pr", [128, 1024], F32))
        po = es.enter_context(nc.psum_tensor("po", [128, 1024], F32))
        pu = es.enter_context(nc.psum_tensor("pu", [128, 512], F32))
        pt = es.enter_context(nc.psum_tensor("pt", [128, 1024], BF16))

        halves = {"pq0": pq[:, 0:512], "pq1": pq[:, 512:1024],
                  "pr0": pr[:, 0:512], "pr1": pr[:, 512:1024],
                  "po0": po[:, 0:512], "po1": po[:, 512:1024]}
        vt = meg[:, 0:8192]
        gs = meg[:, 8192:16384]
        ar = [meg[:, 16384 + i * 4096: 16384 + (i + 1) * 4096] for i in range(4)]
        h1 = meg[:, 0:16384]
        AR = ["ar0", "ar1", "ar2", "ar3"]
        H1K = ["h1_%d" % o for o in range(32)]

        C_PROBE, C_NLN16, C_EPS, C_GNW, C_HALF = 0, 1, 2, 3, 7
        C_RCPF, C_SF, C_R2F, C_RCPP, C_SP, C_R2P = 8, 9, 10, 11, 12, 13
        C_T0, C_T1, C_T2 = 14, 15, 16
        C_DC = 24
        C_SSQ = 32
        C_R = 40
        C_LN = 44

        P.dma("sync", lambda e: e.dma_start(out=id128[:], in_=kb[:, 0:128]), wr=["id128"])
        P.dma("sync", lambda e: e.dma_start(out=Ub[:], in_=kb[:, 128:256]), wr=["Ub"])
        P.dma("sync", lambda e: e.dma_start(out=Lb[:], in_=kb[:, 256:384]), wr=["Lb"])
        P.dma("sync", lambda e: e.dma_start(out=mskT[:], in_=kb[:, 384:512]), wr=["mskT"])
        P.dma("sync", lambda e: e.dma_start(out=cols[:, 0:8], in_=kfc[:, :]), wr=["cols"])
        P.dma("sync", lambda e: e.dma_start(
            out=cols[:, C_GNW:C_GNW + 4],
            in_=w["gnw"].rearrange("o (a p) -> p o a", p=128)[:, 0, :],
            allow_slow_non_contiguous=True),
            rd=["cols"], wr=["cols"], chan="cgnw")
        P.op("vector", lambda e: e.tensor_copy(out=idf32[:], in_=id128[:]),
             rd=["id128"], wr=["idf32"])
        P.op("vector", lambda e: e.memset(ones1[:], 1.0), wr=["ones1"])
        P.op("vector", lambda e: e.tensor_copy(out=qi32[:, 0:1], in_=cols[:, 0:1]),
             rd=["cols"], wr=["qi32"])
        P.op("vector", lambda e: e.tensor_copy(out=cols[:, C_T0:C_T0 + 1], in_=qi32[:, 0:1]),
             rd=["qi32"], wr=["cols"])
        P.op("vector", lambda e: e.tensor_scalar(
            out=cols[:, C_HALF:C_HALF + 1], in0=cols[:, C_T0:C_T0 + 1],
            scalar1=-0.5, op0=ALU.mult, scalar2=1.0, op1=ALU.add),
            rd=["cols"], wr=["cols"])

        P.dma("gpsimd", lambda e: e.dma_start(
            out=wq_s.rearrange("p (o k c) -> p o k c", k=8, c=128),
            in_=w["Wq"].rearrange("(k p) (o c) -> p o k c", p=128, c=128)),
            wr=["wq_s"], chan="wq_s")
        P.dma("gpsimd", lambda e: e.dma_start(
            out=wk_s.rearrange("p (o k c) -> p o k c", k=8, c=128),
            in_=w["Wk"].rearrange("(k p) (o c) -> p o k c", p=128, c=128)),
            wr=["wk_s"], chan="wk_s")
        P.dma("gpsimd", lambda e: e.dma_start(
            out=wv_s.rearrange("p (oh k c) -> p oh k c", k=8, c=512),
            in_=w["Wv"].rearrange("(k p) (oh c) -> p oh k c", p=128, c=512)),
            wr=["wv_s"], chan="wv_s")
        P.dma("gpsimd", lambda e: e.dma_start(
            out=wg_s.rearrange("p (oh k c) -> p oh k c", k=8, c=512),
            in_=w["Wg"].rearrange("(k p) (oh c) -> p oh k c", p=128, c=512)),
            wr=["wg_s"], chan="wg_s")
        P.dma("gpsimd", lambda e: e.dma_start(
            out=wo_s.rearrange("p (oh k c) -> p oh k c", k=16, c=512),
            in_=w["Wo"].rearrange("(k p) (oh c) -> p oh k c", p=128, c=512)),
            wr=["wo_s"], chan="wo_s")

        P.op("vector", lambda e: e.memset(wgk1p[:], 0.0), wr=["wgk1p"])
        P.dma("sync", lambda e: e.dma_start(
            out=xtm[:, 0:8 * 16].rearrange("p (a c) -> p a c", c=16),
            in_=w["Wgk1"].rearrange("(a p) c -> p a c", p=128)),
            rd=["xtm"], wr=["xtm"], chan="sml")
        for k in range(8):
            P.op("vector", lambda e, k=k: e.tensor_copy(
                out=wgk1p[:, k * 128:k * 128 + 16], in_=xtm[:, k * 16:(k + 1) * 16]),
                rd=["xtm"], wr=["wgk1p"])
        P.op("vector", lambda e: e.memset(wgk2a[:], 0.0), wr=["wgk2a"])
        P.dma("sync", lambda e: e.dma_start(out=xtm[0:16, 0:1024], in_=w["Wgk2"][:, :]),
              rd=["xtm"], wr=["xtm"], chan="sml")
        P.op("vector", lambda e: e.tensor_copy(
            out=wgk2a[0:16, 0:1024], in_=xtm[0:16, 0:1024]), rd=["xtm"], wr=["wgk2a"])
        P.dma("sync", lambda e: e.dma_start(out=xtm[0:1, 0:1024], in_=w["bgk2"][:, :]),
              rd=["xtm"], wr=["xtm"], chan="sml")
        P.op("vector", lambda e: e.tensor_copy(
            out=sqs[0:1, 0:1024], in_=xtm[0:1, 0:1024]), rd=["xtm"], wr=["sqs"])
        P.dma("sync", lambda e: e.dma_start(
            out=wgk2a[16:17, 0:1024], in_=sqs[0:1, 0:1024]),
            rd=["sqs"], wr=["wgk2a"], chan="c_wgk2a")
        P.op("vector", lambda e: e.memset(sqs[0:1, 0:512], 1.0), wr=["sqs"])
        for key, t in (("t1a", t1a), ("za", za), ("z2a", z2a)):
            P.op("vector", lambda e, t=t: e.memset(t[:], 0.0), wr=[key])
            P.dma("sync", lambda e, t=t: e.dma_start(
                out=t[16:17, :], in_=sqs[0:1, 0:512]),
                rd=["sqs"], wr=[key], chan="c_" + key)

        def small_T(hbm, ncols, dst, dstkey):
            P.dma("gpsimd", lambda e: e.dma_start(out=x2f32[0:16, 0:ncols], in_=hbm[:, :]),
                  rd=["x2f32"], wr=["x2f32"], chan="qwl")
            P.op("vector", lambda e: e.tensor_copy(
                out=qbf[0:16, 0:ncols if ncols <= 1024 else 1024],
                in_=x2f32[0:16, 0:ncols if ncols <= 1024 else 1024]),
                rd=["x2f32"], wr=["qbf"])
            nb = ncols // 128
            done = 0
            while done < nb:
                if ncols > 1024 and done * 128 % 1024 == 0 and done > 0:
                    off = done * 128
                    P.op("vector", lambda e, off=off: e.tensor_copy(
                        out=qbf[0:16, 0:1024], in_=x2f32[0:16, off:off + 1024]),
                        rd=["x2f32"], wr=["qbf"])
                cnt = min(8, nb - done, (1024 - (done * 128 % 1024)) // 128)
                base = done * 128 % 1024
                P.mm_group([
                    (lambda e, kk=kk, base=base: e.transpose(
                        out=pt[:, kk * 16:(kk + 1) * 16],
                        in_=qbf[0:16, base + kk * 128:base + (kk + 1) * 128],
                        identity=id128[0:16, 0:16]))
                    for kk in range(cnt)], rd=["qbf", "id128"], wr=["pt"])
                P.op("vector", lambda e, done=done, cnt=cnt: e.tensor_copy(
                    out=dst[:, done * 16:(done + cnt) * 16], in_=pt[:, 0:cnt * 16]),
                    rd=["pt"], wr=[dstkey])
                done += cnt

        P.defer = True
        small_T(w["Afc"], 1024, afcT, "afcT")
        small_T(w["Apj"], 4096, apjT, "apjT")

        def quant_weight(hbm, n_out, n_in, qnat, qkey, c_rcp, c_s, c_r2):
            no, ni = n_out // 128, n_in // 128
            hv = hbm.rearrange("(a p) c -> p a c", p=128)
            for o in range(no):
                P.dma("gpsimd", lambda e, o=o: e.dma_start(
                    out=x2f32[:, 0:n_in], in_=hv[:, o, :]), wr=["x2f32"], chan="qwl")
                P.op("vector", lambda e, o=o: e.tensor_reduce(
                    out=stats[:, o:o + 1], in_=x2f32[:, 0:n_in], op=ALU.max, axis=AX.X,
                    apply_absolute_value=True), rd=["x2f32"], wr=["stats"])
            P.op("vector", lambda e, no=no: e.tensor_reduce(
                out=cols[:, C_T1:C_T1 + 1], in_=stats[:, 0:no], op=ALU.max, axis=AX.X),
                rd=["stats"], wr=["cols"])
            P.dma("sync", lambda e: e.dma_start(
                out=x2f32[0:1, 0:128], in_=cols[:, C_T1:C_T1 + 1],
                allow_slow_non_contiguous=True),
                rd=["cols"], wr=["x2f32"], chan="c_amax")
            P.op("vector", lambda e: e.tensor_reduce(
                out=cols[0:1, C_T0:C_T0 + 1], in_=x2f32[0:1, 0:128], op=ALU.max,
                axis=AX.X), rd=["x2f32"], wr=["cols"])
            P.op("vector", lambda e: e.memset(qfp[0:1, 0:128], 1.0), wr=["qfp"])
            P.op("vector", lambda e: e.tensor_scalar(
                out=x2f32[0:1, 0:128], in0=qfp[0:1, 0:128],
                scalar1=cols[0:1, C_T0:C_T0 + 1], op0=ALU.mult, scalar2=None),
                rd=["qfp", "cols"], wr=["x2f32"])
            P.dma("sync", lambda e: e.dma_start(
                out=cols[:, C_T2:C_T2 + 1], in_=x2f32[0:1, 0:128],
                allow_slow_non_contiguous=True),
                rd=["x2f32"], wr=["cols"], chan="c_amax")
            P.op("vector", lambda e: e.tensor_scalar(
                out=cols[:, c_s:c_s + 1], in0=cols[:, C_T2:C_T2 + 1],
                scalar1=1.0 / 127.0, op0=ALU.mult, scalar2=None), rd=["cols"], wr=["cols"])
            P.op("vector", lambda e: e.reciprocal(
                out=cols[:, C_T0:C_T0 + 1], in_=cols[:, C_T2:C_T2 + 1]),
                rd=["cols"], wr=["cols"])
            P.op("vector", lambda e: e.tensor_scalar(
                out=cols[:, c_rcp:c_rcp + 1], in0=cols[:, C_T0:C_T0 + 1],
                scalar1=127.0, op0=ALU.mult, scalar2=None), rd=["cols"], wr=["cols"])
            P.op("vector", lambda e: e.tensor_scalar(
                out=cols[:, c_r2:c_r2 + 1], in0=cols[:, C_T0:C_T0 + 1],
                scalar1=254.0, op0=ALU.mult, scalar2=None), rd=["cols"], wr=["cols"])
            for o in range(no):
                P.dma("gpsimd", lambda e, o=o: e.dma_start(
                    out=x2f32[:, 0:n_in], in_=hv[:, o, :]), wr=["x2f32"], chan="qwl")
                for cg in range(0, n_in, 512):
                    half = (cg // 512) % 2
                    ck = (o * n_in + cg) // 1024
                    pbuf, pkey = (qbf, "qbf") if ck % 2 == 0 else (qbf2, "qbf2")
                    P.op("scalar", lambda e, cg=cg: e.activation(
                        out=sgn[:, 0:512], in_=x2f32[:, cg:cg + 512], func=AF.Sign),
                        rd=["x2f32"], wr=["sgn"])
                    P.op("vector", lambda e: e.tensor_scalar(
                        out=sgn[:, 0:512], in0=sgn[:, 0:512],
                        scalar1=cols[:, C_HALF:C_HALF + 1], op0=ALU.mult, scalar2=None),
                        rd=["sgn", "cols"], wr=["sgn"])
                    P.op("vector", lambda e, cg=cg: e.scalar_tensor_tensor(
                        out=qfp[:, 0:512], in0=x2f32[:, cg:cg + 512],
                        scalar=cols[:, c_rcp:c_rcp + 1], op0=ALU.mult,
                        in1=sgn[:, 0:512], op1=ALU.add),
                        rd=["x2f32", "cols", "sgn"], wr=["qfp"])
                    P.op("vector", lambda e: e.tensor_copy(
                        out=qi32[:, 0:512], in_=qfp[:, 0:512]), rd=["qfp"], wr=["qi32"])
                    P.op("vector", lambda e, half=half, pbuf=pbuf: e.tensor_copy(
                        out=pbuf[:, half * 512:half * 512 + 512], in_=qi32[:, 0:512]),
                        rd=["qi32"], wr=[pkey])
                    if half == 1:
                        P.mm_group([
                            (lambda e, kk=kk, pbuf=pbuf: e.transpose(
                                out=pt[:, kk * 128:(kk + 1) * 128],
                                in_=pbuf[:, kk * 128:(kk + 1) * 128],
                                identity=id128[:]))
                            for kk in range(8)], rd=[pkey, "id128"], wr=["pt"])
                        it2 = (o * n_in + cg) // 1024
                        ek, et = (("qbf2", qbf2) if it2 % 2 == 0 else ("sgn", sgn))
                        P.op("vector", lambda e, et=et: e.tensor_copy(
                            out=et[:, 0:1024], in_=pt[:]), rd=["pt"], wr=[ek])
                        pk = qkey + str(it2 % 2)
                        P.dma("gpsimd", lambda e, o=o, cg=cg, n_in=n_in, et=et: e.dma_start(
                            out=qnat[:, o * n_in + cg - 512:o * n_in + cg + 512],
                            in_=et[:, 0:1024]),
                            rd=[ek], wr=[pk], chan=pk, append=True)

        quant_weight(w["Wfc"], FF, D, wfc_s, "wfc_s", C_RCPF, C_SF, C_R2F)
        quant_weight(w["Wpj"], D, FF, wpj_s, "wpj_s", C_RCPP, C_SP, C_R2P)

        P.op("vector", lambda e: e.memset(fca[:], 0.0), wr=["fca"])
        P.op("vector", lambda e: e.memset(pja[:], 0.0), wr=["pja"])
        bfcv = w["Bfc"].rearrange("(a p) c -> p a c", p=128)
        P.dma("gpsimd", lambda e: e.dma_start(
            out=x2f32[:, 0:512].rearrange("p (a c) -> p a c", c=16), in_=bfcv),
            rd=["x2f32"], wr=["x2f32"], chan="qwl")
        P.op("vector", lambda e: e.tensor_copy(out=qbf[:, 0:512], in_=x2f32[:, 0:512]),
             rd=["x2f32"], wr=["qbf"])
        for ob in range(32):
            P.mm_group([lambda e, ob=ob: e.transpose(
                out=pt[0:16, 0:128], in_=qbf[:, ob * 16:(ob + 1) * 16],
                identity=id128[:])], rd=["qbf", "id128"], wr=["pt"])
            P.op("scalar", lambda e, ob=ob: e.activation(
                out=fca[0:16, ob * 128:(ob + 1) * 128], in_=pt[0:16, 0:128],
                func=AF.Identity, scale=cols[0:16, C_R2F:C_R2F + 1]),
                rd=["pt", "cols"], wr=["fca"])
        P.dma("gpsimd", lambda e: e.dma_start(out=x2f32[0:1, 0:4096], in_=w["bfc"][:, :]),
              rd=["x2f32"], wr=["x2f32"], chan="qwl")
        P.op("scalar", lambda e: e.activation(
            out=qbf[0:1, 0:1024][0:1, 0:1024] if False else qbf[0:1, 0:1024],
            in_=x2f32[0:1, 0:1024], func=AF.Identity,
            scale=cols[0:1, C_RCPF:C_RCPF + 1]), rd=["x2f32", "cols"], wr=["qbf"])
        P.dma("sync", lambda e: e.dma_start(
            out=fca[16:17, 0:1024], in_=qbf[0:1, 0:1024]),
            rd=["qbf"], wr=["fca"], chan="c_fca")
        for qq in range(1, 4):
            P.op("scalar", lambda e, qq=qq: e.activation(
                out=qbf[0:1, 0:1024], in_=x2f32[0:1, qq * 1024:(qq + 1) * 1024],
                func=AF.Identity, scale=cols[0:1, C_RCPF:C_RCPF + 1]),
                rd=["x2f32", "cols"], wr=["qbf"])
            P.dma("sync", lambda e, qq=qq: e.dma_start(
                out=fca[16:17, qq * 1024:(qq + 1) * 1024], in_=qbf[0:1, 0:1024]),
                rd=["qbf"], wr=["fca"], chan="c_fca")

        bpjv = w["Bpj"].rearrange("(a p) c -> p a c", p=128)
        P.dma("gpsimd", lambda e: e.dma_start(
            out=x2f32[:, 0:128].rearrange("p (a c) -> p a c", c=16), in_=bpjv),
            rd=["x2f32"], wr=["x2f32"], chan="qwl")
        P.op("vector", lambda e: e.tensor_copy(out=qbf[:, 0:128], in_=x2f32[:, 0:128]),
             rd=["x2f32"], wr=["qbf"])
        for ob in range(8):
            P.mm_group([lambda e, ob=ob: e.transpose(
                out=pt[0:16, 0:128], in_=qbf[:, ob * 16:(ob + 1) * 16],
                identity=id128[:])], rd=["qbf", "id128"], wr=["pt"])
            P.op("scalar", lambda e, ob=ob: e.activation(
                out=pja[0:16, ob * 128:(ob + 1) * 128], in_=pt[0:16, 0:128],
                func=AF.Identity, scale=cols[0:16, C_R2P:C_R2P + 1]),
                rd=["pt", "cols"], wr=["pja"])
        P.dma("gpsimd", lambda e: e.dma_start(out=x2f32[0:1, 0:1024], in_=w["bpj"][:, :]),
              rd=["x2f32"], wr=["x2f32"], chan="qwl")
        P.op("scalar", lambda e: e.activation(
            out=qbf[0:1, 0:1024], in_=x2f32[0:1, 0:1024], func=AF.Identity,
            scale=cols[0:1, C_RCPP:C_RCPP + 1]), rd=["x2f32", "cols"], wr=["qbf"])
        P.dma("sync", lambda e: e.dma_start(
            out=pja[16:17, 0:1024], in_=qbf[0:1, 0:1024]),
            rd=["qbf"], wr=["pja"], chan="c_pja")

        P.defer = False

        P.inherit(["vt0", "vt1", "vt2", "vt3", "gs0", "gs1", "gs2", "gs3"] + AR,
                  ["xT", "xtm", "sqs"])

        def arena_load(slot_ids, scratch, skey, cols_lo, ncols):
            keys = []
            for i, off in enumerate(range(0, ncols, 4096)):
                si = slot_ids[i]
                key = AR[si]
                keys.append(key)
                qe = "sync" if i % 2 == 0 else "gpsimd"
                P.dma(qe, lambda e, si=si, off=off: e.dma_start(
                    out=ar[si], in_=scratch[:, cols_lo + off:cols_lo + off + 4096]),
                    rd=[skey], wr=[key], chan=key)
            return keys

        qproj_rot = ["pq0", "pq1"]

        for t in range(n_tiles):
            if STAGE < 2:
                break
            P.dma("gpsimd", lambda e, t=t: e.dma_start(
                out=xtm[:].rearrange("p (n c) -> p n c", c=1024),
                in_=xv[:, 4 * t:4 * t + 4, :]), wr=["xtm"], chan="xld")
            for k in range(8):
                bank = qproj_rot[k % 2]
                pb = halves[bank]
                P.mm_group([
                    (lambda e, j=j, k=k, pb=pb: e.transpose(
                        out=pb[:, j * 128:(j + 1) * 128],
                        in_=xtm[:, j * 1024 + k * 128:j * 1024 + (k + 1) * 128],
                        identity=idf32[:]))
                    for j in range(4)], rd=["xtm", "idf32"], wr=[bank])
                eng = "scalar" if k % 2 == 0 else "vector"
                if eng == "scalar":
                    P.op(eng, lambda e, k=k, pb=pb: e.activation(
                        out=xT[:, k * 512:(k + 1) * 512], in_=pb[:], func=AF.Copy,
                        bias=0.0), rd=[bank], wr=["xT"])
                else:
                    P.op(eng, lambda e, k=k, pb=pb: e.tensor_copy(
                        out=xT[:, k * 512:(k + 1) * 512], in_=pb[:]),
                        rd=[bank], wr=["xT"])

            if STAGE < 3:
                continue
            P.pump(PUMP)
            kq = arena_load((0, 1), wq_s, "wq_s", 0, 8192)
            for o in range(8):
                bank = qproj_rot[o % 2]
                pb = halves[bank]
                wkey = kq[0] if o < 4 else kq[1]
                wsl = ar[0] if o < 4 else ar[1]
                ems = []
                for kk in range(8):
                    c0 = (o % 4) * 1024 + kk * 128
                    ems.append(lambda e, pb=pb, wsl=wsl, c0=c0, kk=kk: e.matmul(
                        out=pb[:], lhsT=wsl[:, c0:c0 + 128],
                        rhs=xT[:, kk * 512:(kk + 1) * 512],
                        start=(kk == 0), stop=(kk == 7)))
                P.mm_group(ems, rd=[wkey, "xT"], wr=[bank])
                P.op("vector", lambda e, o=o, pb=pb: e.tensor_copy(
                    out=qf[:, o * 512:(o + 1) * 512], in_=pb[:]), rd=[bank], wr=["qfb"])
            kk_ = arena_load((2, 3), wk_s, "wk_s", 0, 8192)
            for o in range(8):
                bank = qproj_rot[o % 2]
                pb = halves[bank]
                wkey = kk_[0] if o < 4 else kk_[1]
                wsl = ar[2] if o < 4 else ar[3]
                ems = []
                for kk in range(8):
                    c0 = (o % 4) * 1024 + kk * 128
                    ems.append(lambda e, pb=pb, wsl=wsl, c0=c0, kk=kk: e.matmul(
                        out=pb[:], lhsT=wsl[:, c0:c0 + 128],
                        rhs=xT[:, kk * 512:(kk + 1) * 512],
                        start=(kk == 0), stop=(kk == 7)))
                P.mm_group(ems, rd=[wkey, "xT"], wr=[bank])
                P.op("vector", lambda e, o=o, pb=pb: e.tensor_copy(
                    out=kfm[:, o * 512:(o + 1) * 512], in_=pb[:]), rd=[bank], wr=["kfb"])

            if STAGE < 4:
                continue
            P.pump(PUMP)
            P.mm_group([
                (lambda e, kk=kk: e.matmul(
                    out=pu[:], lhsT=wgk1p[:, kk * 128:(kk + 1) * 128],
                    rhs=xT[:, kk * 512:(kk + 1) * 512],
                    start=(kk == 0), stop=(kk == 7)))
                for kk in range(8)], rd=["wgk1p", "xT"], wr=["pu"])
            P.op("vector", lambda e: e.tensor_copy(out=t1a[0:16, :], in_=pu[0:16, :]),
                 rd=["pu"], wr=["t1a"])

            P.pump(PUMP)
            kv = arena_load((0, 1, 2, 3), wv_s, "wv_s", 0, 16384)
            for j in range(4):
                for ohg in range(4):
                    wsl = ar[ohg]
                    wkey = kv[ohg]
                    bank = qproj_rot[(j * 4 + ohg) % 2]
                    pb = halves[bank]
                    ems = []
                    for kk in range(8):
                        rc = kk * 512
                        ems.append(lambda e, pb=pb, wsl=wsl, rc=rc, kk=kk, j=j: e.matmul(
                            out=pb[:], lhsT=xT[:, kk * 512 + j * 128:kk * 512 + (j + 1) * 128],
                            rhs=wsl[:, rc:rc + 512], start=(kk == 0), stop=(kk == 7)))
                    P.mm_group(ems, rd=[wkey, "xT"], wr=[bank])
                    P.op("vector", lambda e, j=j, ohg=ohg, pb=pb: e.tensor_copy(
                        out=vt[:, j * 2048 + ohg * 512:j * 2048 + (ohg + 1) * 512],
                        in_=pb[:]), rd=[bank], wr=["vt%d" % j])
            P.pump(PUMP)
            kg = arena_load((0, 1, 2, 3), wg_s, "wg_s", 0, 16384)
            for j in range(4):
                for ohg in range(4):
                    wsl = ar[ohg]
                    wkey = kg[ohg]
                    bank = qproj_rot[(j * 4 + ohg) % 2]
                    pb = halves[bank]
                    ems = []
                    for kk in range(8):
                        rc = kk * 512
                        ems.append(lambda e, pb=pb, wsl=wsl, rc=rc, kk=kk, j=j: e.matmul(
                            out=pb[:], lhsT=xT[:, kk * 512 + j * 128:kk * 512 + (j + 1) * 128],
                            rhs=wsl[:, rc:rc + 512], start=(kk == 0), stop=(kk == 7)))
                    P.mm_group(ems, rd=[wkey, "xT"], wr=[bank])
                    P.op("scalar", lambda e, j=j, ohg=ohg, pb=pb: e.activation(
                        out=gs[:, j * 2048 + ohg * 512:j * 2048 + (ohg + 1) * 512],
                        in_=pb[:], func=AF.Silu), rd=[bank], wr=["gs%d" % j])

            P.pump(PUMP)
            ko = arena_load((0, 1), wo_s, "wo_s", 0, 8192)
            ko += arena_load((2, 3), wo_s, "wo_s", 8192, 8192)

            if t % 4 == 0:
                P.op("vector", lambda e: e.memset(St[:], 0.0),
                     wr=["S%d" % i for i in range(8)])

            if STAGE < 5:
                continue
            pend = []
            for c in range(4):
                for half in range(2):
                    bank = "pq0" if half == 0 else "pq1"
                    pb = halves[bank]
                    P.mm_group([lambda e, pb=pb, half=half, c=c: e.matmul(
                        out=pb[:], lhsT=t1a[:, c * 128:(c + 1) * 128],
                        rhs=wgk2a[:, half * 512:(half + 1) * 512],
                        start=True, stop=True)], rd=["t1a", "wgk2a"], wr=[bank])
                    P.op("scalar", lambda e, pb=pb, half=half: e.activation(
                        out=spf[:, half * 512:(half + 1) * 512], in_=pb[:],
                        func=AF.Exp, scale=-1.0), rd=[bank], wr=["spf"])
                P.op("scalar", lambda e: e.activation(
                    out=spf[:], in_=spf[:], func=AF.Ln, bias=1.0),
                    rd=["spf"], wr=["spf"])
                P.op("vector", lambda e: e.tensor_copy(out=sphl[:, 0:1024], in_=spf[:]),
                     rd=["spf"], wr=["sphl"])
                P.op("vector", lambda e: e.tensor_sub(
                    sphl[:, 1024:2048], spf[:], sphl[:, 0:1024]),
                    rd=["spf", "sphl"], wr=["sphl"])
                if STAGE < 5.2:
                    continue
                for hb in range(2):
                    bank = "pq0" if hb == 0 else "pq1"
                    pb = halves[bank]
                    ems = []
                    for kk4 in range(4):
                        kk = hb * 4 + kk4
                        ems.append(lambda e, pb=pb, kk4=kk4, kk=kk: e.matmul(
                            out=pb[:, kk4 * 128:(kk4 + 1) * 128],
                            lhsT=sphl[:, kk * 128:(kk + 1) * 128], rhs=Ub[:],
                            start=True, stop=False))
                        ems.append(lambda e, pb=pb, kk4=kk4, kk=kk: e.matmul(
                            out=pb[:, kk4 * 128:(kk4 + 1) * 128],
                            lhsT=sphl[:, 1024 + kk * 128:1024 + (kk + 1) * 128], rhs=Ub[:],
                            start=False, stop=True))
                    P.mm_group(ems, rd=["sphl", "Ub"], wr=[bank])

                if STAGE < 5.4:
                    continue
                ems = []
                for kk in range(8):
                    ems.append(lambda e, kk=kk: e.matmul(
                        out=pu[:, kk:kk + 1], lhsT=sphl[:, kk * 128:(kk + 1) * 128],
                        rhs=ones1[:], start=True, stop=False))
                    ems.append(lambda e, kk=kk: e.matmul(
                        out=pu[:, kk:kk + 1],
                        lhsT=sphl[:, 1024 + kk * 128:1024 + (kk + 1) * 128],
                        rhs=ones1[:], start=False, stop=True))
                P.mm_group(ems, rd=["sphl", "ones1"], wr=["pu"])
                P.op("scalar", lambda e: e.activation(
                    out=cols[:, C_DC:C_DC + 8], in_=pu[:, 0:8], func=AF.Exp,
                    scale=-1.0 / 16.0), rd=["pu"], wr=["cols"])
                for dd in range(8):
                    P.op("vector", lambda e, dd=dd: e.tensor_scalar(
                        out=diagd[:, dd * 128:(dd + 1) * 128], in0=id128[:],
                        scalar1=cols[:, C_DC + dd:C_DC + dd + 1], op0=ALU.mult,
                        scalar2=None), rd=["id128", "cols"], wr=["diagd"])
                for hb in range(2):
                    bank = "pq0" if hb == 0 else "pq1"
                    pb = halves[bank]
                    P.op("scalar", lambda e, pb=pb, hb=hb: e.activation(
                        out=eqc[:, hb * 512:(hb + 1) * 512], in_=pb[:], func=AF.Exp,
                        bias=cols[:, C_NLN16:C_NLN16 + 1]), rd=[bank, "cols"], wr=["eqc"])
                    P.op("scalar", lambda e, pb=pb, hb=hb: e.activation(
                        out=ekc[:, hb * 512:(hb + 1) * 512], in_=pb[:], func=AF.Exp,
                        scale=-1.0), rd=[bank], wr=["ekc"])
                while pend:
                    pend.pop(0)()
                if STAGE < 5.6:
                    continue
                for half in range(2):
                    bank = "pq0" if half == 0 else "pq1"
                    pb = halves[bank]
                    P.mm_group([
                        (lambda e, pb=pb, half=half: e.matmul(
                            out=pb[:], lhsT=Lb[:], rhs=sphl[:, half * 512:(half + 1) * 512],
                            start=True, stop=False)),
                        (lambda e, pb=pb, half=half: e.matmul(
                            out=pb[:], lhsT=Lb[:],
                            rhs=sphl[:, 1024 + half * 512:1024 + (half + 1) * 512],
                            start=False, stop=True))], rd=["sphl", "Lb"], wr=[bank])
                    P.op("scalar", lambda e, pb=pb, half=half: e.activation(
                        out=erc[:, half * 512:(half + 1) * 512], in_=pb[:], func=AF.Exp),
                        rd=[bank], wr=["erc"])
                if STAGE < 6:
                    continue
                qv3 = qf[:].rearrange("p (o v) -> p o v", v=512)
                kv3 = kfm[:].rearrange("p (o v) -> p o v", v=512)
                e3 = lambda buf: buf[:].rearrange("p (o v) -> p o v", v=128)
                P.op("vector", lambda e, c=c: e.tensor_tensor(
                    out=e3(qg), in0=qv3[:, :, c * 128:(c + 1) * 128], in1=e3(eqc),
                    op=ALU.mult), rd=["qfb", "eqc"], wr=["qg"])
                P.op("vector", lambda e, c=c: e.tensor_tensor(
                    out=e3(ki), in0=kv3[:, :, c * 128:(c + 1) * 128], in1=e3(ekc),
                    op=ALU.mult), rd=["kfb", "ekc"], wr=["ki"])
                P.mm_group([
                    (lambda e, kk=kk, c=c: e.transpose(
                        out=pt[:, kk * 128:(kk + 1) * 128],
                        in_=kfm[:, kk * 512 + c * 128:kk * 512 + (c + 1) * 128],
                        identity=id128[:]))
                    for kk in range(8)], rd=["kfb", "id128"], wr=["pt"])
                P.op("vector", lambda e: e.tensor_copy(out=kt[:], in_=pt[:]),
                     rd=["pt"], wr=["kt"])
                P.op("vector", lambda e: e.tensor_mul(kt[:], kt[:], erc[:]),
                     rd=["kt", "erc"], wr=["kt"])
                if STAGE < 7:
                    continue
                for h in range(4):
                    P.mm_group([
                        (lambda e, h=h, d=d: e.matmul(
                            out=pu[:, h * 128:(h + 1) * 128],
                            lhsT=ki[:, (2 * h + d) * 128:(2 * h + d + 1) * 128],
                            rhs=qg[:, (2 * h + d) * 128:(2 * h + d + 1) * 128],
                            start=(d == 0), stop=(d == 1)))
                        for d in range(2)], rd=["ki", "qg"], wr=["pu"])
                    P.op("vector", lambda e, h=h: e.tensor_mul(
                        scm[:, h * 128:(h + 1) * 128], pu[:, h * 128:(h + 1) * 128],
                        mskT[:]), rd=["pu", "mskT"], wr=["scm"])
                obank = ["po0", "po1", "pq0", "pq1"]
                for h in range(4):
                    bank = obank[h]
                    pb = halves[bank]
                    ems = [(lambda e, h=h, c=c, pb=pb: e.matmul(
                        out=pb[:], lhsT=scm[:, h * 128:(h + 1) * 128],
                        rhs=vt[:, c * 2048 + h * 512:c * 2048 + (h + 1) * 512],
                        start=True, stop=False))]
                    for d in range(2):
                        ems.append(lambda e, h=h, d=d, pb=pb: e.matmul(
                            out=pb[:], lhsT=qg[:, (2 * h + d) * 128:(2 * h + d + 1) * 128],
                            rhs=St[:, (2 * h + d) * 512:(2 * h + d + 1) * 512],
                            start=False, stop=(d == 1)))
                    P.mm_group(ems, rd=["scm", "qg", "vt%d" % c,
                                        "S%d" % (2 * h), "S%d" % (2 * h + 1)], wr=[bank])
                    P.op("scalar", lambda e, pb=pb, h=h: e.activation(
                        out=sqs[:, 0:512], in_=pb[:], func=AF.Square,
                        accum_out=cols[:, C_SSQ + h:C_SSQ + h + 1]),
                        rd=[bank], wr=["sqs", "cols"])
                P.op("scalar", lambda e: e.activation(
                    out=cols[:, C_R:C_R + 4], in_=cols[:, C_SSQ:C_SSQ + 4],
                    func=AF.Ln, scale=1.0 / 512.0, bias=cols[:, C_EPS:C_EPS + 1]),
                    rd=["cols"], wr=["cols"])
                P.op("scalar", lambda e: e.activation(
                    out=cols[:, C_R:C_R + 4], in_=cols[:, C_R:C_R + 4],
                    func=AF.Exp, scale=-0.5), rd=["cols"], wr=["cols"])
                for h in range(4):
                    bank = obank[h]
                    pb = halves[bank]
                    P.op("vector", lambda e, pb=pb, h=h, c=c: e.scalar_tensor_tensor(
                        out=og[:, h * 512:(h + 1) * 512], in0=pb[:],
                        scalar=cols[:, C_R + h:C_R + h + 1], op0=ALU.mult,
                        in1=gs[:, c * 2048 + h * 512:c * 2048 + (h + 1) * 512],
                        op1=ALU.mult), rd=[bank, "cols", "gs%d" % c], wr=["og"])
                if STAGE < 8:
                    continue
                for h in range(4):
                    for d in range(2):
                        sb = "po0" if (h * 2 + d) % 2 == 0 else "po1"
                        pb = halves[sb]
                        sk = "S%d" % (2 * h + d)
                        dd = 2 * h + d
                        P.mm_group([
                            (lambda e, pb=pb, sk=sk, dd=dd: e.matmul(
                                out=pb[:], lhsT=diagd[:, dd * 128:(dd + 1) * 128],
                                rhs=St[:, dd * 512:(dd + 1) * 512],
                                start=True, stop=False)),
                            (lambda e, pb=pb, h=h, d=d, c=c: e.matmul(
                                out=pb[:], lhsT=kt[:, (2 * h + d) * 128:(2 * h + d + 1) * 128],
                                rhs=vt[:, c * 2048 + h * 512:c * 2048 + (h + 1) * 512],
                                start=False, stop=True))],
                            rd=["diagd", "kt", "vt%d" % c, sk], wr=[sb])
                        P.op("vector", lambda e, pb=pb, h=h, d=d: e.tensor_copy(
                            out=St[:, (2 * h + d) * 512:(2 * h + d + 1) * 512],
                            in_=pb[:]), rd=[sb], wr=[sk])
                if STAGE < 9:
                    continue
                for g2 in range(2):
                    P.mm_group([
                        (lambda e, b=b, g2=g2: e.transpose(
                            out=pt[:, b * 128:(b + 1) * 128],
                            in_=og[:, g2 * 1024 + b * 128:g2 * 1024 + (b + 1) * 128],
                            identity=id128[:]))
                        for b in range(8)], rd=["og", "id128"], wr=["pt"])
                    for j in range(4):
                        pv = pt[:].rearrange("p (b c) -> p b c", c=128)[:, j::4, :]
                        xv2 = xT[:, g2 * 1024:(g2 + 1) * 1024].rearrange(
                            "p (b c) -> p b c", c=128)[:, j::4, :]
                        P.op("vector", lambda e, pv=pv, xv2=xv2, j=j: e.tensor_scalar(
                            out=xv2, in0=pv, scalar1=cols[:, C_GNW + j:C_GNW + j + 1],
                            op0=ALU.mult, scalar2=None),
                            rd=["pt", "cols"], wr=["xT"])
                if STAGE < 10:
                    continue
                for oh in range(2):
                    bank = "pr0" if oh == 0 else "pr1"
                    pb = halves[bank]
                    ems = []
                    for dvk in range(16):
                        wsl = ar[(oh * 2) + (dvk // 8)]
                        rc = (dvk % 8) * 512
                        ems.append(lambda e, pb=pb, wsl=wsl, rc=rc, dvk=dvk: e.matmul(
                            out=pb[:], lhsT=xT[:, dvk * 128:(dvk + 1) * 128],
                            rhs=wsl[:, rc:rc + 512], start=(dvk == 0), stop=(dvk == 15)))
                    P.mm_group(ems, rd=["xT"] + ko, wr=[bank])
                P.op("vector", lambda e, c=c: e.scalar_tensor_tensor(
                    out=x1r[:], in0=pr[:], scalar=1.0, op0=ALU.mult,
                    in1=xtm[:, c * 1024:(c + 1) * 1024], op1=ALU.add,
                    accum_out=cols[:, C_LN:C_LN + 1]),
                    rd=["pr0", "pr1", "xtm", ], wr=["x1r", "cols"])
                P.op("scalar", lambda e: e.activation(
                    out=sqs[:, 0:1024], in_=x1r[:], func=AF.Square,
                    accum_out=cols[:, C_LN + 1:C_LN + 2]),
                    rd=["x1r"], wr=["sqs", "cols"])
                P.op("vector", lambda e: e.tensor_scalar(
                    out=cols[:, C_LN + 2:C_LN + 3], in0=cols[:, C_LN:C_LN + 1],
                    scalar1=1.0 / 1024.0, op0=ALU.mult, scalar2=None),
                    rd=["cols"], wr=["cols"])
                P.op("vector", lambda e: e.tensor_mul(
                    cols[:, C_LN + 3:C_LN + 4], cols[:, C_LN + 2:C_LN + 3],
                    cols[:, C_LN + 2:C_LN + 3]), rd=["cols"], wr=["cols"])
                P.op("vector", lambda e: e.tensor_scalar(
                    out=cols[:, C_LN + 4:C_LN + 5], in0=cols[:, C_LN + 1:C_LN + 2],
                    scalar1=1.0 / 1024.0, op0=ALU.mult,
                    scalar2=cols[:, C_LN + 3:C_LN + 4], op1=ALU.subtract),
                    rd=["cols"], wr=["cols"])
                P.op("scalar", lambda e: e.activation(
                    out=cols[:, C_LN + 5:C_LN + 6], in_=cols[:, C_LN + 4:C_LN + 5],
                    func=AF.Ln, bias=cols[:, C_EPS:C_EPS + 1]), rd=["cols"], wr=["cols"])
                P.op("scalar", lambda e: e.activation(
                    out=cols[:, C_LN + 5:C_LN + 6], in_=cols[:, C_LN + 5:C_LN + 6],
                    func=AF.Exp, scale=-0.5), rd=["cols"], wr=["cols"])
                P.op("vector", lambda e: e.tensor_scalar(
                    out=x1c[:], in0=x1r[:], scalar1=cols[:, C_LN + 2:C_LN + 3],
                    op0=ALU.subtract, scalar2=cols[:, C_LN + 5:C_LN + 6],
                    op1=ALU.mult), rd=["x1r", "cols"], wr=["x1c"])
                xbslice = x1bf[:, (c % 2) * 1024:(c % 2) * 1024 + 1024]
                P.op("vector", lambda e, xbslice=xbslice: e.tensor_scalar(
                    out=xbslice, in0=x1r[:], scalar1=cols[:, C_LN + 2:C_LN + 3],
                    op0=ALU.subtract, scalar2=cols[:, C_LN + 5:C_LN + 6],
                    op1=ALU.mult), rd=["x1r", "cols"], wr=["x1bf%d" % (c % 2)])
                P.dma("gpsimd", lambda e, t=t, c=c: e.dma_start(
                    out=x1v[:, 4 * t + c, :], in_=x1c[:]), rd=["x1c"], wr=["x1_s"],
                    chan="x1st", append=True)
                def _x1T(c=c, xbslice=xbslice):
                    P.mm_group([
                        (lambda e, kk=kk: e.transpose(
                            out=pt[:, kk * 128:(kk + 1) * 128],
                            in_=xbslice[:, kk * 128:(kk + 1) * 128], identity=id128[:]))
                        for kk in range(8)], rd=["x1bf%d" % (c % 2), "id128"], wr=["pt"])
                    xts = x1Tt[:, (c % 2) * 1024:(c % 2) * 1024 + 1024]
                    P.op("vector", lambda e, xts=xts: e.tensor_copy(
                        out=xts, in_=pt[:]), rd=["pt"], wr=["x1Tt%d" % (c % 2)])
                    P.dma("gpsimd", lambda e, t=t, c=c, xts=xts: e.dma_start(
                        out=x1T_s[:, t * 4096:(t + 1) * 4096].rearrange(
                            "p (k v) -> p k v", v=512)[:, :, c * 128:(c + 1) * 128],
                        in_=xts.rearrange("p (k v) -> p k v", v=128)),
                        rd=["x1Tt%d" % (c % 2)], wr=["x1T_s%d" % (c % 2)],
                        chan="x1Tst%d" % (c % 2), append=True)
                pend.append(_x1T)
            while pend:
                pend.pop(0)()

        P.flush_bg()

        if not skip_b:
            P.inherit(H1K + AR, ["vt%d" % j for j in range(4)] +
                      ["gs%d" % j for j in range(4)] + AR + ["xT"])
            for t in range(n_b_tiles):
                P.dma("gpsimd", lambda e, t=t: e.dma_start(
                    out=qf[:], in_=x1T_s[:, t * 4096:(t + 1) * 4096]),
                    rd=["x1T_s0", "x1T_s1"], wr=["qfb"], chan="x1Tld")
                P.dma("gpsimd", lambda e, t=t: e.dma_start(
                    out=xtm[:].rearrange("p (n c) -> p n c", c=1024),
                    in_=x1v[:, 4 * t:4 * t + 4, :]), rd=["x1_s"], wr=["xtm"],
                    chan="x1ld")
                P.mm_group([
                    (lambda e, kk=kk: e.matmul(
                        out=pu[0:16, :], lhsT=afcT[:, kk * 16:(kk + 1) * 16],
                        rhs=qf[:, kk * 512:(kk + 1) * 512],
                        start=(kk == 0), stop=(kk == 7)))
                    for kk in range(8)], rd=["afcT", "qfb"], wr=["pu"])
                P.op("vector", lambda e: e.tensor_copy(out=za[0:16, :], in_=pu[0:16, :]),
                     rd=["pu"], wr=["za"])
                for oc in range(8):
                    slot = ar[oc % 4]
                    skey = AR[oc % 4]
                    qe = "sync" if oc % 2 == 0 else "gpsimd"
                    P.dma(qe, lambda e, oc=oc, slot=slot: e.dma_start(
                        out=slot, in_=wfc_s[:, oc * 4096:(oc + 1) * 4096]),
                        rd=["wfc_s0", "wfc_s1"], wr=[skey], chan=skey)
                    for o4 in range(4):
                        o = oc * 4 + o4
                        bank = qproj_rot[o % 2]
                        pb = halves[bank]
                        ems = []
                        for kk in range(8):
                            ems.append(lambda e, pb=pb, slot=slot, o4=o4, kk=kk: e.matmul(
                                out=pb[:], lhsT=slot[:, o4 * 1024 + kk * 128:o4 * 1024 + (kk + 1) * 128],
                                rhs=qf[:, kk * 512:(kk + 1) * 512],
                                start=(kk == 0), stop=False))
                        ems.append(lambda e, pb=pb, o=o: e.matmul(
                            out=pb[:], lhsT=fca[:, o * 128:(o + 1) * 128],
                            rhs=za[:], start=False, stop=True))
                        P.mm_group(ems, rd=[skey, "qfb", "fca", "za"], wr=[bank])
                        P.op("scalar", lambda e, pb=pb, o=o: e.activation(
                            out=h1[:, o * 512:(o + 1) * 512], in_=pb[:], func=AF.Gelu,
                            scale=cols[:, C_SF:C_SF + 1]),
                            rd=[bank, "cols"], wr=["h1_%d" % o])
                P.mm_group([
                    (lambda e, kk=kk: e.matmul(
                        out=pu[0:16, :], lhsT=apjT[:, kk * 16:(kk + 1) * 16],
                        rhs=h1[:, kk * 512:(kk + 1) * 512],
                        start=(kk == 0), stop=(kk == 31)))
                    for kk in range(32)], rd=H1K + ["apjT"], wr=["pu"])
                P.op("vector", lambda e: e.tensor_copy(out=z2a[0:16, :], in_=pu[0:16, :]),
                     rd=["pu"], wr=["z2a"])
                for o in range(8):
                    slot = ar[o % 4]
                    skey = AR[o % 4]
                    qe = "sync" if o % 2 == 0 else "gpsimd"
                    P.dma(qe, lambda e, o=o, slot=slot: e.dma_start(
                        out=slot, in_=wpj_s[:, o * 4096:(o + 1) * 4096]),
                        rd=["wpj_s0", "wpj_s1"], wr=[skey], chan=skey)
                    bank = ["po0", "po1"][o % 2]
                    pb = halves[bank]
                    ems = []
                    for kk in range(32):
                        ems.append(lambda e, pb=pb, slot=slot, kk=kk: e.matmul(
                            out=pb[:], lhsT=slot[:, kk * 128:(kk + 1) * 128],
                            rhs=h1[:, kk * 512:(kk + 1) * 512],
                            start=(kk == 0), stop=False))
                    ems.append(lambda e, pb=pb, o=o: e.matmul(
                        out=pb[:], lhsT=pja[:, o * 128:(o + 1) * 128], rhs=z2a[:],
                        start=False, stop=True))
                    P.mm_group(ems, rd=[skey, "z2a", "pja"] + H1K, wr=[bank])
                    P.op("vector", lambda e, pb=pb, o=o: e.tensor_copy(
                        out=x2f32[:, o * 512:(o + 1) * 512], in_=pb[:]),
                        rd=[bank], wr=["x2f32"])
                for j in range(4):
                    for hb in range(2):
                        bank = "pq0" if hb == 0 else "pq1"
                        pb = halves[bank]
                        P.mm_group([
                            (lambda e, pb=pb, o4=o4, hb=hb, j=j: e.transpose(
                                out=pb[:, o4 * 128:(o4 + 1) * 128],
                                in_=x2f32[:, (hb * 4 + o4) * 512 + j * 128:
                                          (hb * 4 + o4) * 512 + (j + 1) * 128],
                                identity=idf32[:]))
                            for o4 in range(4)], rd=["x2f32", "idf32"], wr=[bank])
                    P.op("vector", lambda e, j=j: e.scalar_tensor_tensor(
                        out=x1r[:], in0=pq[:], scalar=cols[:, C_SP:C_SP + 1],
                        op0=ALU.mult, in1=xtm[:, j * 1024:(j + 1) * 1024], op1=ALU.add,
                        accum_out=cols[:, C_LN:C_LN + 1]),
                        rd=["pq0", "pq1", "xtm", "cols"], wr=["x1r", "cols"])
                    P.op("scalar", lambda e: e.activation(
                        out=sqs[:, 0:1024], in_=x1r[:], func=AF.Square,
                        accum_out=cols[:, C_LN + 1:C_LN + 2]),
                        rd=["x1r"], wr=["sqs", "cols"])
                    P.op("vector", lambda e: e.tensor_scalar(
                        out=cols[:, C_LN + 2:C_LN + 3], in0=cols[:, C_LN:C_LN + 1],
                        scalar1=1.0 / 1024.0, op0=ALU.mult, scalar2=None),
                        rd=["cols"], wr=["cols"])
                    P.op("vector", lambda e: e.tensor_mul(
                        cols[:, C_LN + 3:C_LN + 4], cols[:, C_LN + 2:C_LN + 3],
                        cols[:, C_LN + 2:C_LN + 3]), rd=["cols"], wr=["cols"])
                    P.op("vector", lambda e: e.tensor_scalar(
                        out=cols[:, C_LN + 4:C_LN + 5], in0=cols[:, C_LN + 1:C_LN + 2],
                        scalar1=1.0 / 1024.0, op0=ALU.mult,
                        scalar2=cols[:, C_LN + 3:C_LN + 4], op1=ALU.subtract),
                        rd=["cols"], wr=["cols"])
                    P.op("scalar", lambda e: e.activation(
                        out=cols[:, C_LN + 5:C_LN + 6], in_=cols[:, C_LN + 4:C_LN + 5],
                        func=AF.Ln, bias=cols[:, C_EPS:C_EPS + 1]), rd=["cols"], wr=["cols"])
                    P.op("scalar", lambda e: e.activation(
                        out=cols[:, C_LN + 5:C_LN + 6], in_=cols[:, C_LN + 5:C_LN + 6],
                        func=AF.Exp, scale=-0.5), rd=["cols"], wr=["cols"])
                    P.op("vector", lambda e: e.tensor_scalar(
                        out=x1c[:], in0=x1r[:], scalar1=cols[:, C_LN + 2:C_LN + 3],
                        op0=ALU.subtract, scalar2=cols[:, C_LN + 5:C_LN + 6],
                        op1=ALU.mult), rd=["x1r", "cols"], wr=["x1c"])
                    P.dma("gpsimd", lambda e, t=t, j=j: e.dma_start(
                        out=yv[:, 4 * t + j, :], in_=x1c[:]), rd=["x1c"], wr=["y_h"],
                        chan="yst", append=True)

        for ename in ENGINES:
            P.barrier(ename)

        import os as _os
        if _os.environ.get("PROBE_SBUF"):
            lo, hi = 0, 131072
            pidx = 0
            while lo < hi:
                mid = (lo + hi + 1) // 2
                pidx += 1
                try:
                    with nc.sbuf_tensor("probe%d" % pidx, [128, mid], mybir.dt.int8):
                        lo = mid
                except AssertionError:
                    hi = mid - 1
            print("SBUF headroom bytes/partition:", lo)
        block = es.enter_context(nc.Block())
        P.emit(nc, es, block)

    return nc


def host_constants():
    import ml_dtypes
    bf = ml_dtypes.bfloat16
    kb = np.zeros((128, 512), dtype=bf)
    kb[:, 0:128] = np.eye(128, dtype=np.float32).astype(bf)
    jj, ii = np.meshgrid(np.arange(128), np.arange(128), indexing="ij")
    kb[:, 128:256] = np.where(jj <= ii, np.float32(-1.0 / 16.0), 0.0).astype(bf)
    kb[:, 256:384] = np.where(jj > ii, np.float32(-1.0 / 16.0), 0.0).astype(bf)
    kb[:, 384:512] = np.where(jj <= ii, np.float32(1.0), 0.0).astype(bf)
    kfc = np.zeros((128, 8), dtype=np.float32)
    kfc[:, 0] = 1.5
    kfc[:, 1] = NEGLN16
    kfc[:, 2] = EPS
    return kb, kfc


def kernel(**inputs):
    kb, kfc = host_constants()
    nc = build_program()
    names = {
        "Wq": "L0_attn_Wq", "Wk": "L0_attn_Wk", "Wv": "L0_attn_Wv",
        "Wg": "L0_attn_Wg", "Wgk1": "L0_attn_Wgk1", "Wgk2": "L0_attn_Wgk2",
        "bgk2": "L0_attn_bgk2", "gnw": "L0_attn_g_norm_w", "Wo": "L0_attn_Wo",
        "Wfc": "L0_ffn_W_fc", "bfc": "L0_ffn_b_fc", "Afc": "L0_ffn_A_fc",
        "Bfc": "L0_ffn_B_fc", "Wpj": "L0_ffn_W_proj", "bpj": "L0_ffn_b_proj",
        "Apj": "L0_ffn_A_proj", "Bpj": "L0_ffn_B_proj",
    }
    x = np.ascontiguousarray(inputs["L0_attn_hidden_states"])
    in_maps = []
    for core in range(8):
        m = {"xin": x[2 * core:2 * core + 2].reshape(NTOK, D), "kb": kb, "kfc": kfc}
        for k, nm in names.items():
            a = inputs[nm]
            if a.ndim == 1:
                a = a.reshape(1, -1)
            m[k] = np.ascontiguousarray(a)
        in_maps.append(m)
    res = run_bass_kernel_spmd(nc, in_maps, core_ids=list(range(8)))
    out = np.concatenate([res.results[c]["y"].reshape(2, 2048, D) for c in range(8)],
                         axis=0)
    return out.astype(np.float32)


# version 16
# speedup vs baseline: 1.0450x; 1.0188x over previous
import contextlib
import math
import numpy as np
import concourse.bass as bass
import concourse.mybir as mybir
from concourse.bass_utils import run_bass_kernel_spmd

F32 = mybir.dt.float32
BF16 = mybir.dt.bfloat16
I32 = mybir.dt.int32
AF = mybir.ActivationFunctionType
ALU = mybir.AluOpType
AX = mybir.AxisListType

D = 1024
DK = 1024
DV = 2048
NH = 4
FF = 4096
LR = 16
T = 512
NTILES = 8
NTOK = NTILES * T
NEGLN16 = -math.log(16.0)
EPS = 1e-5

ENGINES = ("sync", "scalar", "vector", "tensor", "gpsimd")
STAGE = 99.0
PUMP = 32
BGN = 10**9


class Plan:

    def __init__(self):
        self.ops = {e: [] for e in ENGINES}
        self.counts = {}
        self.watermark = {}
        self.writer = {}
        self.readers = {}
        self.sem_names = []
        self.defer = False
        self.bgq = []

    def _sem(self, name):
        if name not in self.counts:
            self.counts[name] = 0
            self.sem_names.append(name)
        return name

    def _edges(self, ename, rd, wr, skip_war=False):
        need = {}
        def add(tok):
            s, v = tok
            if v > need.get(s, 0):
                need[s] = v
        for b in rd:
            t = self.writer.get(b)
            if t:
                add(t)
        for b in wr:
            if not skip_war:
                t = self.writer.get(b)
                if t:
                    add(t)
                for rt in self.readers.get(b, ()):
                    add(rt)
        waits = []
        for s, v in need.items():
            if self.watermark.get((ename, s), 0) < v:
                self.watermark[(ename, s)] = v
                waits.append((s, v))
        return waits

    def _post(self, tok, rd, wr):
        for b in wr:
            self.writer[b] = tok
            self.readers[b] = []
        for b in rd:
            self.readers.setdefault(b, []).append(tok)

    def op(self, ename, emit, rd=(), wr=()):
        if self.defer:
            self.bgq.append(("op", (ename, emit, rd, wr)))
            return
        waits = self._edges(ename, rd, wr)
        s = self._sem("e_" + ename)
        self.counts[s] += 1
        self.ops[ename].append((waits, emit, (s, 1)))
        self._post((s, self.counts[s]), rd, wr)

    def mm_group(self, emits, rd=(), wr=()):
        if self.defer:
            self.bgq.append(("mm", (emits, rd, wr)))
            return
        waits = self._edges("tensor", rd, wr)
        s = self._sem("e_tensor")
        self.counts[s] += 1
        n = len(emits)
        for i, em in enumerate(emits):
            self.ops["tensor"].append(
                (waits if i == 0 else [], em, (s, 1) if i == n - 1 else None))
        self._post((s, self.counts[s]), rd, wr)

    def dma(self, ename, emit, rd=(), wr=(), chan=None, append=False):
        if self.defer:
            self.bgq.append(("dma", (ename, emit, rd, wr, chan, append)))
            return
        waits = self._edges(ename, rd, wr, skip_war=append)
        chan = self._sem("d_" + (chan or (wr[0] if wr else rd[0])))
        self.counts[chan] += 16
        self.ops[ename].append((waits, emit, (chan, 16)))
        self._post((chan, self.counts[chan]), rd, wr)

    def pump(self, n):
        assert not self.defer
        if not hasattr(self, "npumped"):
            self.npumped = 0
        n = min(n, BGN - self.npumped)
        if n <= 0:
            return
        self.npumped += n
        for _ in range(min(n, len(self.bgq))):
            kind, args = self.bgq.pop(0)
            if kind == "op":
                self.op(*args)
            elif kind == "mm":
                self.mm_group(*args)
            else:
                ename, emit, rd, wr, chan, append = args
                self.dma(ename, emit, rd=rd, wr=wr, chan=chan, append=append)

    def flush_bg(self):
        while self.bgq:
            kind, args = self.bgq.pop(0)
            if kind == "op":
                self.op(*args)
            elif kind == "mm":
                self.mm_group(*args)
            else:
                ename, emit, rd, wr, chan, append = args
                self.dma(ename, emit, rd=rd, wr=wr, chan=chan, append=append)

    def inherit(self, dst_bufs, src_bufs):
        toks = []
        for b in src_bufs:
            if b in self.writer:
                toks.append(self.writer[b])
            toks.extend(self.readers.get(b, ()))
        for b in dst_bufs:
            for t in toks:
                self.readers.setdefault(b, []).append(t)

    def barrier(self, ename):
        waits = []
        for s in self.sem_names:
            v = self.counts[s]
            if v > 0 and self.watermark.get((ename, s), 0) < v:
                self.watermark[(ename, s)] = v
                waits.append((s, v))
        self.ops[ename].append((waits, None, None))

    def emit(self, nc, es, block):
        sems = {s: es.enter_context(nc.semaphore(s)) for s in self.sem_names}
        for ename in ENGINES:
            oplist = self.ops[ename]
            if not oplist:
                continue
            def body(eng, oplist=oplist):
                for waits, emitfn, inc in oplist:
                    for s, v in waits:
                        eng.wait_ge(sems[s], v)
                    if emitfn is not None:
                        inst = emitfn(eng)
                        if inc is not None:
                            inst.then_inc(sems[inc[0]], inc[1])
            getattr(block, ename)(body)


def build_program(n_tiles=NTILES, n_b_tiles=None, skip_b=False):
    if n_b_tiles is None:
        n_b_tiles = n_tiles
    nc = bass.Bass("TRN2", target_bir_lowering=False)
    P = Plan()

    xin = nc.dram_tensor("xin", [NTOK, D], F32, kind="ExternalInput").ap()
    w = {}
    for name, shape in [
        ("Wq", [D, DK]), ("Wk", [D, DK]), ("Wv", [D, DV]), ("Wg", [D, DV]),
        ("Wgk1", [D, LR]), ("Wgk2", [LR, DK]), ("bgk2", [1, DK]),
        ("gnw", [1, 512]), ("Wo", [DV, D]),
        ("Wfc", [FF, D]), ("bfc", [1, FF]), ("Afc", [LR, D]), ("Bfc", [FF, LR]),
        ("Wpj", [D, FF]), ("bpj", [1, D]), ("Apj", [LR, FF]), ("Bpj", [D, LR]),
    ]:
        w[name] = nc.dram_tensor(name, shape, F32, kind="ExternalInput").ap()
    kb = nc.dram_tensor("kb", [128, 512], BF16, kind="ExternalInput").ap()
    kfc = nc.dram_tensor("kfc", [128, 8], F32, kind="ExternalInput").ap()
    yout = nc.dram_tensor("y", [NTOK, D], F32, kind="ExternalOutput").ap()

    wq_s = nc.dram_tensor("wq_s", [128, 8 * 1024], BF16, kind="Internal").ap()
    wk_s = nc.dram_tensor("wk_s", [128, 8 * 1024], BF16, kind="Internal").ap()
    wv_s = nc.dram_tensor("wv_s", [128, 16384], BF16, kind="Internal").ap()
    wg_s = nc.dram_tensor("wg_s", [128, 16384], BF16, kind="Internal").ap()
    wo_s = nc.dram_tensor("wo_s", [128, 16384], BF16, kind="Internal").ap()
    wfc_s = nc.dram_tensor("wfc_s", [128, 32768], BF16, kind="Internal").ap()
    wpj_s = nc.dram_tensor("wpj_s", [128, 32768], BF16, kind="Internal").ap()
    wfcq_s = nc.dram_tensor("wfcq_s", [128, 32768], BF16, kind="Internal").ap()
    wpjq_s = nc.dram_tensor("wpjq_s", [128, 32768], BF16, kind="Internal").ap()
    x1_s = nc.dram_tensor("x1_s", [NTOK, D], F32, kind="Internal").ap()
    x1T_s = nc.dram_tensor("x1T_s", [128, 8 * 4096], BF16, kind="Internal").ap()

    xv = xin.rearrange("(n p) c -> p n c", p=128)
    yv = yout.rearrange("(n p) c -> p n c", p=128)
    x1v = x1_s.rearrange("(n p) c -> p n c", p=128)

    es = contextlib.ExitStack()
    with es:
        S_ = lambda name, shape, dt=BF16: es.enter_context(nc.sbuf_tensor(name, shape, dt))
        xtm = S_("xtm", [128, 4096], F32)
        x1r = S_("x1r", [128, 1024], F32)
        x1c = S_("x1c", [128, 1024], F32)
        spf = S_("spf", [128, 1024], F32)
        x2f32 = S_("x2f32", [128, 4096], F32)
        cols = S_("cols", [128, 64], F32)
        stats = S_("stats", [128, 32], F32)
        sqs = S_("sqs", [128, 1024])
        sgn = S_("sgn", [128, 1024])
        qi32 = S_("qi32", [128, 512], I32)
        qfp = S_("qfp", [128, 512], F32)
        qbf = S_("qbf", [128, 1024])
        qbf2 = S_("qbf2", [128, 1024])
        xT = S_("xT", [128, 4096])
        qf = S_("qf", [128, 4096])
        kfm = S_("kfm", [128, 4096])
        meg = S_("meg", [128, 32768])
        sphl = S_("sphl", [128, 2048])
        eqc = S_("eqc", [128, 1024])
        ekc = S_("ekc", [128, 1024])
        erc = S_("erc", [128, 1024])
        qg = S_("qg", [128, 1024])
        ki = S_("ki", [128, 1024])
        kt = S_("kt", [128, 1024])
        St = S_("St", [128, 4096])
        scm = S_("scm", [128, 512])
        og = S_("og", [128, 2048])
        x1bf = S_("x1bf", [128, 2048])
        x1Tt = S_("x1Tt", [128, 4096])
        diagd = S_("diagd", [128, 1024])
        wgk1p = S_("wgk1p", [128, 1024])
        wgk2a = S_("wgk2a", [128, 1024])
        afcT = S_("afcT", [128, 128])
        apjT = S_("apjT", [128, 512])
        fca = S_("fca", [128, 4096])
        pja = S_("pja", [128, 1024])
        t1a = S_("t1a", [128, 512])
        za = S_("za", [128, 512])
        z2a = S_("z2a", [128, 512])
        id128 = S_("id128", [128, 128])
        idf32 = S_("idf32", [128, 128], F32)
        ones1 = S_("ones1", [128, 1])
        Ub = S_("Ubm", [128, 128])
        Lb = S_("Lbm", [128, 128])
        mskT = S_("mskT", [128, 128])

        pq = es.enter_context(nc.psum_tensor("pq", [128, 1024], F32))
        pr = es.enter_context(nc.psum_tensor("pr", [128, 1024], F32))
        po = es.enter_context(nc.psum_tensor("po", [128, 1024], F32))
        pu = es.enter_context(nc.psum_tensor("pu", [128, 512], F32))
        pt = es.enter_context(nc.psum_tensor("pt", [128, 1024], BF16))

        halves = {"pq0": pq[:, 0:512], "pq1": pq[:, 512:1024],
                  "pr0": pr[:, 0:512], "pr1": pr[:, 512:1024],
                  "po0": po[:, 0:512], "po1": po[:, 512:1024]}
        vt = meg[:, 0:8192]
        gs = meg[:, 8192:16384]
        ar = [meg[:, 16384 + i * 4096: 16384 + (i + 1) * 4096] for i in range(4)]
        h1 = meg[:, 0:16384]
        AR = ["ar0", "ar1", "ar2", "ar3"]
        H1K = ["h1_%d" % o for o in range(32)]

        C_PROBE, C_NLN16, C_EPS, C_GNW, C_HALF = 0, 1, 2, 3, 7
        C_RCPF, C_SF, C_R2F, C_RCPP, C_SP, C_R2P = 8, 9, 10, 11, 12, 13
        C_T0, C_T1, C_T2 = 14, 15, 16
        C_DC = 24
        C_SSQ = 32
        C_R = 40
        C_LN = 44

        P.dma("sync", lambda e: e.dma_start(out=id128[:], in_=kb[:, 0:128]), wr=["id128"])
        P.dma("sync", lambda e: e.dma_start(out=Ub[:], in_=kb[:, 128:256]), wr=["Ub"])
        P.dma("sync", lambda e: e.dma_start(out=Lb[:], in_=kb[:, 256:384]), wr=["Lb"])
        P.dma("sync", lambda e: e.dma_start(out=mskT[:], in_=kb[:, 384:512]), wr=["mskT"])
        P.dma("sync", lambda e: e.dma_start(out=cols[:, 0:8], in_=kfc[:, :]), wr=["cols"])
        P.dma("sync", lambda e: e.dma_start(
            out=cols[:, C_GNW:C_GNW + 4],
            in_=w["gnw"].rearrange("o (a p) -> p o a", p=128)[:, 0, :],
            allow_slow_non_contiguous=True),
            rd=["cols"], wr=["cols"], chan="cgnw")
        P.op("vector", lambda e: e.tensor_copy(out=idf32[:], in_=id128[:]),
             rd=["id128"], wr=["idf32"])
        P.op("vector", lambda e: e.memset(ones1[:], 1.0), wr=["ones1"])
        P.op("vector", lambda e: e.tensor_copy(out=qi32[:, 0:1], in_=cols[:, 0:1]),
             rd=["cols"], wr=["qi32"])
        P.op("vector", lambda e: e.tensor_copy(out=cols[:, C_T0:C_T0 + 1], in_=qi32[:, 0:1]),
             rd=["qi32"], wr=["cols"])
        P.op("vector", lambda e: e.tensor_scalar(
            out=cols[:, C_HALF:C_HALF + 1], in0=cols[:, C_T0:C_T0 + 1],
            scalar1=-0.5, op0=ALU.mult, scalar2=1.0, op1=ALU.add),
            rd=["cols"], wr=["cols"])

        P.dma("gpsimd", lambda e: e.dma_start(
            out=wq_s.rearrange("p (o k c) -> p o k c", k=8, c=128),
            in_=w["Wq"].rearrange("(k p) (o c) -> p o k c", p=128, c=128)),
            wr=["wq_s"], chan="wq_s")
        P.dma("gpsimd", lambda e: e.dma_start(
            out=wk_s.rearrange("p (o k c) -> p o k c", k=8, c=128),
            in_=w["Wk"].rearrange("(k p) (o c) -> p o k c", p=128, c=128)),
            wr=["wk_s"], chan="wk_s")
        P.dma("gpsimd", lambda e: e.dma_start(
            out=wv_s.rearrange("p (oh k c) -> p oh k c", k=8, c=512),
            in_=w["Wv"].rearrange("(k p) (oh c) -> p oh k c", p=128, c=512)),
            wr=["wv_s"], chan="wv_s")
        P.dma("gpsimd", lambda e: e.dma_start(
            out=wg_s.rearrange("p (oh k c) -> p oh k c", k=8, c=512),
            in_=w["Wg"].rearrange("(k p) (oh c) -> p oh k c", p=128, c=512)),
            wr=["wg_s"], chan="wg_s")
        P.dma("gpsimd", lambda e: e.dma_start(
            out=wo_s.rearrange("p (oh k c) -> p oh k c", k=16, c=512),
            in_=w["Wo"].rearrange("(k p) (oh c) -> p oh k c", p=128, c=512)),
            wr=["wo_s"], chan="wo_s")

        P.op("vector", lambda e: e.memset(wgk1p[:], 0.0), wr=["wgk1p"])
        P.dma("sync", lambda e: e.dma_start(
            out=xtm[:, 0:8 * 16].rearrange("p (a c) -> p a c", c=16),
            in_=w["Wgk1"].rearrange("(a p) c -> p a c", p=128)),
            rd=["xtm"], wr=["xtm"], chan="sml")
        for k in range(8):
            P.op("vector", lambda e, k=k: e.tensor_copy(
                out=wgk1p[:, k * 128:k * 128 + 16], in_=xtm[:, k * 16:(k + 1) * 16]),
                rd=["xtm"], wr=["wgk1p"])
        P.op("vector", lambda e: e.memset(wgk2a[:], 0.0), wr=["wgk2a"])
        P.dma("sync", lambda e: e.dma_start(out=xtm[0:16, 0:1024], in_=w["Wgk2"][:, :]),
              rd=["xtm"], wr=["xtm"], chan="sml")
        P.op("vector", lambda e: e.tensor_copy(
            out=wgk2a[0:16, 0:1024], in_=xtm[0:16, 0:1024]), rd=["xtm"], wr=["wgk2a"])
        P.dma("sync", lambda e: e.dma_start(out=xtm[0:1, 0:1024], in_=w["bgk2"][:, :]),
              rd=["xtm"], wr=["xtm"], chan="sml")
        P.op("vector", lambda e: e.tensor_copy(
            out=sqs[0:1, 0:1024], in_=xtm[0:1, 0:1024]), rd=["xtm"], wr=["sqs"])
        P.dma("sync", lambda e: e.dma_start(
            out=wgk2a[16:17, 0:1024], in_=sqs[0:1, 0:1024]),
            rd=["sqs"], wr=["wgk2a"], chan="c_wgk2a")
        P.op("vector", lambda e: e.memset(sqs[0:1, 0:512], 1.0), wr=["sqs"])
        for key, t in (("t1a", t1a), ("za", za), ("z2a", z2a)):
            P.op("vector", lambda e, t=t: e.memset(t[:], 0.0), wr=[key])
            P.dma("sync", lambda e, t=t: e.dma_start(
                out=t[16:17, :], in_=sqs[0:1, 0:512]),
                rd=["sqs"], wr=[key], chan="c_" + key)

        def small_T(hbm, ncols, dst, dstkey):
            P.dma("gpsimd", lambda e: e.dma_start(out=x2f32[0:16, 0:ncols], in_=hbm[:, :]),
                  rd=["x2f32"], wr=["x2f32"], chan="qwl")
            P.op("vector", lambda e: e.tensor_copy(
                out=qbf[0:16, 0:ncols if ncols <= 1024 else 1024],
                in_=x2f32[0:16, 0:ncols if ncols <= 1024 else 1024]),
                rd=["x2f32"], wr=["qbf"])
            nb = ncols // 128
            done = 0
            while done < nb:
                if ncols > 1024 and done * 128 % 1024 == 0 and done > 0:
                    off = done * 128
                    P.op("vector", lambda e, off=off: e.tensor_copy(
                        out=qbf[0:16, 0:1024], in_=x2f32[0:16, off:off + 1024]),
                        rd=["x2f32"], wr=["qbf"])
                cnt = min(8, nb - done, (1024 - (done * 128 % 1024)) // 128)
                base = done * 128 % 1024
                P.mm_group([
                    (lambda e, kk=kk, base=base: e.transpose(
                        out=pt[:, kk * 16:(kk + 1) * 16],
                        in_=qbf[0:16, base + kk * 128:base + (kk + 1) * 128],
                        identity=id128[0:16, 0:16]))
                    for kk in range(cnt)], rd=["qbf", "id128"], wr=["pt"])
                P.op("vector", lambda e, done=done, cnt=cnt: e.tensor_copy(
                    out=dst[:, done * 16:(done + cnt) * 16], in_=pt[:, 0:cnt * 16]),
                    rd=["pt"], wr=[dstkey])
                done += cnt

        P.defer = True
        small_T(w["Afc"], 1024, afcT, "afcT")
        small_T(w["Apj"], 4096, apjT, "apjT")

        def quant_weight(hbm, n_out, n_in, qnat, qkey, c_rcp, c_s, c_r2):
            no, ni = n_out // 128, n_in // 128
            hv = hbm.rearrange("(a p) c -> p a c", p=128)
            for o in range(no):
                P.dma("gpsimd", lambda e, o=o: e.dma_start(
                    out=x2f32[:, 0:n_in], in_=hv[:, o, :]), wr=["x2f32"], chan="qwl")
                P.op("vector", lambda e, o=o: e.tensor_reduce(
                    out=stats[:, o:o + 1], in_=x2f32[:, 0:n_in], op=ALU.max, axis=AX.X,
                    apply_absolute_value=True), rd=["x2f32"], wr=["stats"])
            P.op("vector", lambda e, no=no: e.tensor_reduce(
                out=cols[:, C_T1:C_T1 + 1], in_=stats[:, 0:no], op=ALU.max, axis=AX.X),
                rd=["stats"], wr=["cols"])
            P.dma("sync", lambda e: e.dma_start(
                out=x2f32[0:1, 0:128], in_=cols[:, C_T1:C_T1 + 1],
                allow_slow_non_contiguous=True),
                rd=["cols"], wr=["x2f32"], chan="c_amax")
            P.op("vector", lambda e: e.tensor_reduce(
                out=cols[0:1, C_T0:C_T0 + 1], in_=x2f32[0:1, 0:128], op=ALU.max,
                axis=AX.X), rd=["x2f32"], wr=["cols"])
            P.op("vector", lambda e: e.memset(qfp[0:1, 0:128], 1.0), wr=["qfp"])
            P.op("vector", lambda e: e.tensor_scalar(
                out=x2f32[0:1, 0:128], in0=qfp[0:1, 0:128],
                scalar1=cols[0:1, C_T0:C_T0 + 1], op0=ALU.mult, scalar2=None),
                rd=["qfp", "cols"], wr=["x2f32"])
            P.dma("sync", lambda e: e.dma_start(
                out=cols[:, C_T2:C_T2 + 1], in_=x2f32[0:1, 0:128],
                allow_slow_non_contiguous=True),
                rd=["x2f32"], wr=["cols"], chan="c_amax")
            P.op("vector", lambda e: e.tensor_scalar(
                out=cols[:, c_s:c_s + 1], in0=cols[:, C_T2:C_T2 + 1],
                scalar1=1.0 / 127.0, op0=ALU.mult, scalar2=None), rd=["cols"], wr=["cols"])
            P.op("vector", lambda e: e.reciprocal(
                out=cols[:, C_T0:C_T0 + 1], in_=cols[:, C_T2:C_T2 + 1]),
                rd=["cols"], wr=["cols"])
            P.op("vector", lambda e: e.tensor_scalar(
                out=cols[:, c_rcp:c_rcp + 1], in0=cols[:, C_T0:C_T0 + 1],
                scalar1=127.0, op0=ALU.mult, scalar2=None), rd=["cols"], wr=["cols"])
            P.op("vector", lambda e: e.tensor_scalar(
                out=cols[:, c_r2:c_r2 + 1], in0=cols[:, C_T0:C_T0 + 1],
                scalar1=254.0, op0=ALU.mult, scalar2=None), rd=["cols"], wr=["cols"])
            for o in range(no):
                P.dma("gpsimd", lambda e, o=o: e.dma_start(
                    out=x2f32[:, 0:n_in], in_=hv[:, o, :]), wr=["x2f32"], chan="qwl")
                for cg in range(0, n_in, 512):
                    half = (cg // 512) % 2
                    ck = (o * n_in + cg) // 1024
                    pbuf, pkey = (qbf, "qbf") if ck % 2 == 0 else (qbf2, "qbf2")
                    P.op("scalar", lambda e, cg=cg: e.activation(
                        out=sgn[:, 0:512], in_=x2f32[:, cg:cg + 512], func=AF.Sign),
                        rd=["x2f32"], wr=["sgn"])
                    P.op("vector", lambda e: e.tensor_scalar(
                        out=sgn[:, 0:512], in0=sgn[:, 0:512],
                        scalar1=cols[:, C_HALF:C_HALF + 1], op0=ALU.mult, scalar2=None),
                        rd=["sgn", "cols"], wr=["sgn"])
                    P.op("vector", lambda e, cg=cg: e.scalar_tensor_tensor(
                        out=qfp[:, 0:512], in0=x2f32[:, cg:cg + 512],
                        scalar=cols[:, c_rcp:c_rcp + 1], op0=ALU.mult,
                        in1=sgn[:, 0:512], op1=ALU.add),
                        rd=["x2f32", "cols", "sgn"], wr=["qfp"])
                    P.op("vector", lambda e: e.tensor_copy(
                        out=qi32[:, 0:512], in_=qfp[:, 0:512]), rd=["qfp"], wr=["qi32"])
                    P.op("vector", lambda e, half=half, pbuf=pbuf: e.tensor_copy(
                        out=pbuf[:, half * 512:half * 512 + 512], in_=qi32[:, 0:512]),
                        rd=["qi32"], wr=[pkey])
                    if half == 1:
                        P.mm_group([
                            (lambda e, kk=kk, pbuf=pbuf: e.transpose(
                                out=pt[:, kk * 128:(kk + 1) * 128],
                                in_=pbuf[:, kk * 128:(kk + 1) * 128],
                                identity=id128[:]))
                            for kk in range(8)], rd=[pkey, "id128"], wr=["pt"])
                        it2 = (o * n_in + cg) // 1024
                        ek, et = (("qbf2", qbf2) if it2 % 2 == 0 else ("sgn", sgn))
                        P.op("vector", lambda e, et=et: e.tensor_copy(
                            out=et[:, 0:1024], in_=pt[:]), rd=["pt"], wr=[ek])
                        pk = qkey + str(it2 % 2)
                        P.dma("gpsimd", lambda e, o=o, cg=cg, n_in=n_in, et=et: e.dma_start(
                            out=qnat[:, o * n_in + cg - 512:o * n_in + cg + 512],
                            in_=et[:, 0:1024]),
                            rd=[ek], wr=[pk], chan=pk, append=True)

        quant_weight(w["Wfc"], FF, D, wfc_s, "wfc_s", C_RCPF, C_SF, C_R2F)
        quant_weight(w["Wpj"], D, FF, wpj_s, "wpj_s", C_RCPP, C_SP, C_R2P)

        P.op("vector", lambda e: e.memset(fca[:], 0.0), wr=["fca"])
        P.op("vector", lambda e: e.memset(pja[:], 0.0), wr=["pja"])
        bfcv = w["Bfc"].rearrange("(a p) c -> p a c", p=128)
        P.dma("gpsimd", lambda e: e.dma_start(
            out=x2f32[:, 0:512].rearrange("p (a c) -> p a c", c=16), in_=bfcv),
            rd=["x2f32"], wr=["x2f32"], chan="qwl")
        P.op("vector", lambda e: e.tensor_copy(out=qbf[:, 0:512], in_=x2f32[:, 0:512]),
             rd=["x2f32"], wr=["qbf"])
        for ob in range(32):
            P.mm_group([lambda e, ob=ob: e.transpose(
                out=pt[0:16, 0:128], in_=qbf[:, ob * 16:(ob + 1) * 16],
                identity=id128[:])], rd=["qbf", "id128"], wr=["pt"])
            P.op("scalar", lambda e, ob=ob: e.activation(
                out=fca[0:16, ob * 128:(ob + 1) * 128], in_=pt[0:16, 0:128],
                func=AF.Identity, scale=cols[0:16, C_R2F:C_R2F + 1]),
                rd=["pt", "cols"], wr=["fca"])
        P.dma("gpsimd", lambda e: e.dma_start(out=x2f32[0:1, 0:4096], in_=w["bfc"][:, :]),
              rd=["x2f32"], wr=["x2f32"], chan="qwl")
        P.op("scalar", lambda e: e.activation(
            out=qbf[0:1, 0:1024][0:1, 0:1024] if False else qbf[0:1, 0:1024],
            in_=x2f32[0:1, 0:1024], func=AF.Identity,
            scale=cols[0:1, C_RCPF:C_RCPF + 1]), rd=["x2f32", "cols"], wr=["qbf"])
        P.dma("sync", lambda e: e.dma_start(
            out=fca[16:17, 0:1024], in_=qbf[0:1, 0:1024]),
            rd=["qbf"], wr=["fca"], chan="c_fca")
        for qq in range(1, 4):
            P.op("scalar", lambda e, qq=qq: e.activation(
                out=qbf[0:1, 0:1024], in_=x2f32[0:1, qq * 1024:(qq + 1) * 1024],
                func=AF.Identity, scale=cols[0:1, C_RCPF:C_RCPF + 1]),
                rd=["x2f32", "cols"], wr=["qbf"])
            P.dma("sync", lambda e, qq=qq: e.dma_start(
                out=fca[16:17, qq * 1024:(qq + 1) * 1024], in_=qbf[0:1, 0:1024]),
                rd=["qbf"], wr=["fca"], chan="c_fca")

        bpjv = w["Bpj"].rearrange("(a p) c -> p a c", p=128)
        P.dma("gpsimd", lambda e: e.dma_start(
            out=x2f32[:, 0:128].rearrange("p (a c) -> p a c", c=16), in_=bpjv),
            rd=["x2f32"], wr=["x2f32"], chan="qwl")
        P.op("vector", lambda e: e.tensor_copy(out=qbf[:, 0:128], in_=x2f32[:, 0:128]),
             rd=["x2f32"], wr=["qbf"])
        for ob in range(8):
            P.mm_group([lambda e, ob=ob: e.transpose(
                out=pt[0:16, 0:128], in_=qbf[:, ob * 16:(ob + 1) * 16],
                identity=id128[:])], rd=["qbf", "id128"], wr=["pt"])
            P.op("scalar", lambda e, ob=ob: e.activation(
                out=pja[0:16, ob * 128:(ob + 1) * 128], in_=pt[0:16, 0:128],
                func=AF.Identity, scale=cols[0:16, C_R2P:C_R2P + 1]),
                rd=["pt", "cols"], wr=["pja"])
        P.dma("gpsimd", lambda e: e.dma_start(out=x2f32[0:1, 0:1024], in_=w["bpj"][:, :]),
              rd=["x2f32"], wr=["x2f32"], chan="qwl")
        P.op("scalar", lambda e: e.activation(
            out=qbf[0:1, 0:1024], in_=x2f32[0:1, 0:1024], func=AF.Identity,
            scale=cols[0:1, C_RCPP:C_RCPP + 1]), rd=["x2f32", "cols"], wr=["qbf"])
        P.dma("sync", lambda e: e.dma_start(
            out=pja[16:17, 0:1024], in_=qbf[0:1, 0:1024]),
            rd=["qbf"], wr=["pja"], chan="c_pja")

        P.defer = False

        P.inherit(["vt0", "vt1", "vt2", "vt3", "gs0", "gs1", "gs2", "gs3"] + AR,
                  ["xT", "xtm", "sqs"])

        def arena_load(slot_ids, scratch, skey, cols_lo, ncols):
            keys = []
            for i, off in enumerate(range(0, ncols, 4096)):
                si = slot_ids[i]
                key = AR[si]
                keys.append(key)
                qe = "sync" if i % 2 == 0 else "gpsimd"
                P.dma(qe, lambda e, si=si, off=off: e.dma_start(
                    out=ar[si], in_=scratch[:, cols_lo + off:cols_lo + off + 4096]),
                    rd=[skey], wr=[key], chan=key)
            return keys

        qproj_rot = ["pq0", "pq1"]

        for t in range(n_tiles):
            if STAGE < 2:
                break
            P.dma("sync", lambda e, t=t: e.dma_start(
                out=xtm[:].rearrange("p (n c) -> p n c", c=1024),
                in_=xv[:, 4 * t:4 * t + 4, :]), wr=["xtm"], chan="xld")
            for k in range(8):
                bank = qproj_rot[k % 2]
                pb = halves[bank]
                P.mm_group([
                    (lambda e, j=j, k=k, pb=pb: e.transpose(
                        out=pb[:, j * 128:(j + 1) * 128],
                        in_=xtm[:, j * 1024 + k * 128:j * 1024 + (k + 1) * 128],
                        identity=idf32[:]))
                    for j in range(4)], rd=["xtm", "idf32"], wr=[bank])
                eng = "scalar" if k % 2 == 0 else "vector"
                if eng == "scalar":
                    P.op(eng, lambda e, k=k, pb=pb: e.activation(
                        out=xT[:, k * 512:(k + 1) * 512], in_=pb[:], func=AF.Copy,
                        bias=0.0), rd=[bank], wr=["xT"])
                else:
                    P.op(eng, lambda e, k=k, pb=pb: e.tensor_copy(
                        out=xT[:, k * 512:(k + 1) * 512], in_=pb[:]),
                        rd=[bank], wr=["xT"])

            if STAGE < 3:
                continue
            P.pump(PUMP)
            kq = arena_load((0, 1), wq_s, "wq_s", 0, 8192)
            for o in range(8):
                bank = qproj_rot[o % 2]
                pb = halves[bank]
                wkey = kq[0] if o < 4 else kq[1]
                wsl = ar[0] if o < 4 else ar[1]
                ems = []
                for kk in range(8):
                    c0 = (o % 4) * 1024 + kk * 128
                    ems.append(lambda e, pb=pb, wsl=wsl, c0=c0, kk=kk: e.matmul(
                        out=pb[:], lhsT=wsl[:, c0:c0 + 128],
                        rhs=xT[:, kk * 512:(kk + 1) * 512],
                        start=(kk == 0), stop=(kk == 7)))
                P.mm_group(ems, rd=[wkey, "xT"], wr=[bank])
                P.op("vector", lambda e, o=o, pb=pb: e.tensor_copy(
                    out=qf[:, o * 512:(o + 1) * 512], in_=pb[:]), rd=[bank], wr=["qfb"])
            kk_ = arena_load((2, 3), wk_s, "wk_s", 0, 8192)
            for o in range(8):
                bank = qproj_rot[o % 2]
                pb = halves[bank]
                wkey = kk_[0] if o < 4 else kk_[1]
                wsl = ar[2] if o < 4 else ar[3]
                ems = []
                for kk in range(8):
                    c0 = (o % 4) * 1024 + kk * 128
                    ems.append(lambda e, pb=pb, wsl=wsl, c0=c0, kk=kk: e.matmul(
                        out=pb[:], lhsT=wsl[:, c0:c0 + 128],
                        rhs=xT[:, kk * 512:(kk + 1) * 512],
                        start=(kk == 0), stop=(kk == 7)))
                P.mm_group(ems, rd=[wkey, "xT"], wr=[bank])
                P.op("vector", lambda e, o=o, pb=pb: e.tensor_copy(
                    out=kfm[:, o * 512:(o + 1) * 512], in_=pb[:]), rd=[bank], wr=["kfb"])

            if STAGE < 4:
                continue
            P.pump(PUMP)
            P.mm_group([
                (lambda e, kk=kk: e.matmul(
                    out=pu[:], lhsT=wgk1p[:, kk * 128:(kk + 1) * 128],
                    rhs=xT[:, kk * 512:(kk + 1) * 512],
                    start=(kk == 0), stop=(kk == 7)))
                for kk in range(8)], rd=["wgk1p", "xT"], wr=["pu"])
            P.op("vector", lambda e: e.tensor_copy(out=t1a[0:16, :], in_=pu[0:16, :]),
                 rd=["pu"], wr=["t1a"])

            P.pump(PUMP)
            kv = arena_load((0, 1, 2, 3), wv_s, "wv_s", 0, 16384)
            for j in range(4):
                for ohg in range(4):
                    wsl = ar[ohg]
                    wkey = kv[ohg]
                    bank = qproj_rot[(j * 4 + ohg) % 2]
                    pb = halves[bank]
                    ems = []
                    for kk in range(8):
                        rc = kk * 512
                        ems.append(lambda e, pb=pb, wsl=wsl, rc=rc, kk=kk, j=j: e.matmul(
                            out=pb[:], lhsT=xT[:, kk * 512 + j * 128:kk * 512 + (j + 1) * 128],
                            rhs=wsl[:, rc:rc + 512], start=(kk == 0), stop=(kk == 7)))
                    P.mm_group(ems, rd=[wkey, "xT"], wr=[bank])
                    P.op("vector", lambda e, j=j, ohg=ohg, pb=pb: e.tensor_copy(
                        out=vt[:, j * 2048 + ohg * 512:j * 2048 + (ohg + 1) * 512],
                        in_=pb[:]), rd=[bank], wr=["vt%d" % j])
            P.pump(PUMP)
            kg = arena_load((0, 1, 2, 3), wg_s, "wg_s", 0, 16384)
            for j in range(4):
                for ohg in range(4):
                    wsl = ar[ohg]
                    wkey = kg[ohg]
                    bank = qproj_rot[(j * 4 + ohg) % 2]
                    pb = halves[bank]
                    ems = []
                    for kk in range(8):
                        rc = kk * 512
                        ems.append(lambda e, pb=pb, wsl=wsl, rc=rc, kk=kk, j=j: e.matmul(
                            out=pb[:], lhsT=xT[:, kk * 512 + j * 128:kk * 512 + (j + 1) * 128],
                            rhs=wsl[:, rc:rc + 512], start=(kk == 0), stop=(kk == 7)))
                    P.mm_group(ems, rd=[wkey, "xT"], wr=[bank])
                    P.op("scalar", lambda e, j=j, ohg=ohg, pb=pb: e.activation(
                        out=gs[:, j * 2048 + ohg * 512:j * 2048 + (ohg + 1) * 512],
                        in_=pb[:], func=AF.Silu), rd=[bank], wr=["gs%d" % j])

            P.pump(PUMP)
            ko = arena_load((0, 1), wo_s, "wo_s", 0, 8192)
            ko += arena_load((2, 3), wo_s, "wo_s", 8192, 8192)

            if t % 4 == 0:
                P.op("vector", lambda e: e.memset(St[:], 0.0),
                     wr=["S%d" % i for i in range(8)])

            if STAGE < 5:
                continue
            pend = []
            for c in range(4):
                for half in range(2):
                    bank = "pq0" if half == 0 else "pq1"
                    pb = halves[bank]
                    P.mm_group([lambda e, pb=pb, half=half, c=c: e.matmul(
                        out=pb[:], lhsT=t1a[:, c * 128:(c + 1) * 128],
                        rhs=wgk2a[:, half * 512:(half + 1) * 512],
                        start=True, stop=True)], rd=["t1a", "wgk2a"], wr=[bank])
                    P.op("scalar", lambda e, pb=pb, half=half: e.activation(
                        out=spf[:, half * 512:(half + 1) * 512], in_=pb[:],
                        func=AF.Exp, scale=-1.0), rd=[bank], wr=["spf"])
                P.op("scalar", lambda e: e.activation(
                    out=spf[:], in_=spf[:], func=AF.Ln, bias=1.0),
                    rd=["spf"], wr=["spf"])
                P.op("vector", lambda e: e.tensor_copy(out=sphl[:, 0:1024], in_=spf[:]),
                     rd=["spf"], wr=["sphl"])
                P.op("vector", lambda e: e.tensor_sub(
                    sphl[:, 1024:2048], spf[:], sphl[:, 0:1024]),
                    rd=["spf", "sphl"], wr=["sphl"])
                if STAGE < 5.2:
                    continue
                for hb in range(2):
                    bank = "pq0" if hb == 0 else "pq1"
                    pb = halves[bank]
                    ems = []
                    for kk4 in range(4):
                        kk = hb * 4 + kk4
                        ems.append(lambda e, pb=pb, kk4=kk4, kk=kk: e.matmul(
                            out=pb[:, kk4 * 128:(kk4 + 1) * 128],
                            lhsT=sphl[:, kk * 128:(kk + 1) * 128], rhs=Ub[:],
                            start=True, stop=False))
                        ems.append(lambda e, pb=pb, kk4=kk4, kk=kk: e.matmul(
                            out=pb[:, kk4 * 128:(kk4 + 1) * 128],
                            lhsT=sphl[:, 1024 + kk * 128:1024 + (kk + 1) * 128], rhs=Ub[:],
                            start=False, stop=True))
                    P.mm_group(ems, rd=["sphl", "Ub"], wr=[bank])

                if STAGE < 5.4:
                    continue
                ems = []
                for kk in range(8):
                    ems.append(lambda e, kk=kk: e.matmul(
                        out=pu[:, kk:kk + 1], lhsT=sphl[:, kk * 128:(kk + 1) * 128],
                        rhs=ones1[:], start=True, stop=False))
                    ems.append(lambda e, kk=kk: e.matmul(
                        out=pu[:, kk:kk + 1],
                        lhsT=sphl[:, 1024 + kk * 128:1024 + (kk + 1) * 128],
                        rhs=ones1[:], start=False, stop=True))
                P.mm_group(ems, rd=["sphl", "ones1"], wr=["pu"])
                P.op("scalar", lambda e: e.activation(
                    out=cols[:, C_DC:C_DC + 8], in_=pu[:, 0:8], func=AF.Exp,
                    scale=-1.0 / 16.0), rd=["pu"], wr=["cols"])
                for dd in range(8):
                    P.op("vector", lambda e, dd=dd: e.tensor_scalar(
                        out=diagd[:, dd * 128:(dd + 1) * 128], in0=id128[:],
                        scalar1=cols[:, C_DC + dd:C_DC + dd + 1], op0=ALU.mult,
                        scalar2=None), rd=["id128", "cols"], wr=["diagd"])
                for hb in range(2):
                    bank = "pq0" if hb == 0 else "pq1"
                    pb = halves[bank]
                    P.op("scalar", lambda e, pb=pb, hb=hb: e.activation(
                        out=eqc[:, hb * 512:(hb + 1) * 512], in_=pb[:], func=AF.Exp,
                        bias=cols[:, C_NLN16:C_NLN16 + 1]), rd=[bank, "cols"], wr=["eqc"])
                    P.op("scalar", lambda e, pb=pb, hb=hb: e.activation(
                        out=ekc[:, hb * 512:(hb + 1) * 512], in_=pb[:], func=AF.Exp,
                        scale=-1.0), rd=[bank], wr=["ekc"])
                while pend:
                    pend.pop(0)()
                if STAGE < 5.6:
                    continue
                for half in range(2):
                    bank = "pq0" if half == 0 else "pq1"
                    pb = halves[bank]
                    P.mm_group([
                        (lambda e, pb=pb, half=half: e.matmul(
                            out=pb[:], lhsT=Lb[:], rhs=sphl[:, half * 512:(half + 1) * 512],
                            start=True, stop=False)),
                        (lambda e, pb=pb, half=half: e.matmul(
                            out=pb[:], lhsT=Lb[:],
                            rhs=sphl[:, 1024 + half * 512:1024 + (half + 1) * 512],
                            start=False, stop=True))], rd=["sphl", "Lb"], wr=[bank])
                    P.op("scalar", lambda e, pb=pb, half=half: e.activation(
                        out=erc[:, half * 512:(half + 1) * 512], in_=pb[:], func=AF.Exp),
                        rd=[bank], wr=["erc"])
                if STAGE < 6:
                    continue
                qv3 = qf[:].rearrange("p (o v) -> p o v", v=512)
                kv3 = kfm[:].rearrange("p (o v) -> p o v", v=512)
                e3 = lambda buf: buf[:].rearrange("p (o v) -> p o v", v=128)
                P.op("vector", lambda e, c=c: e.tensor_tensor(
                    out=e3(qg), in0=qv3[:, :, c * 128:(c + 1) * 128], in1=e3(eqc),
                    op=ALU.mult), rd=["qfb", "eqc"], wr=["qg"])
                P.op("vector", lambda e, c=c: e.tensor_tensor(
                    out=e3(ki), in0=kv3[:, :, c * 128:(c + 1) * 128], in1=e3(ekc),
                    op=ALU.mult), rd=["kfb", "ekc"], wr=["ki"])
                P.mm_group([
                    (lambda e, kk=kk, c=c: e.transpose(
                        out=pt[:, kk * 128:(kk + 1) * 128],
                        in_=kfm[:, kk * 512 + c * 128:kk * 512 + (c + 1) * 128],
                        identity=id128[:]))
                    for kk in range(8)], rd=["kfb", "id128"], wr=["pt"])
                P.op("vector", lambda e: e.tensor_copy(out=kt[:], in_=pt[:]),
                     rd=["pt"], wr=["kt"])
                P.op("vector", lambda e: e.tensor_mul(kt[:], kt[:], erc[:]),
                     rd=["kt", "erc"], wr=["kt"])
                if STAGE < 7:
                    continue
                for h in range(4):
                    P.mm_group([
                        (lambda e, h=h, d=d: e.matmul(
                            out=pu[:, h * 128:(h + 1) * 128],
                            lhsT=ki[:, (2 * h + d) * 128:(2 * h + d + 1) * 128],
                            rhs=qg[:, (2 * h + d) * 128:(2 * h + d + 1) * 128],
                            start=(d == 0), stop=(d == 1)))
                        for d in range(2)], rd=["ki", "qg"], wr=["pu"])
                    P.op("vector", lambda e, h=h: e.tensor_mul(
                        scm[:, h * 128:(h + 1) * 128], pu[:, h * 128:(h + 1) * 128],
                        mskT[:]), rd=["pu", "mskT"], wr=["scm"])
                obank = ["po0", "po1", "pq0", "pq1"]
                for h in range(4):
                    bank = obank[h]
                    pb = halves[bank]
                    ems = [(lambda e, h=h, c=c, pb=pb: e.matmul(
                        out=pb[:], lhsT=scm[:, h * 128:(h + 1) * 128],
                        rhs=vt[:, c * 2048 + h * 512:c * 2048 + (h + 1) * 512],
                        start=True, stop=False))]
                    for d in range(2):
                        ems.append(lambda e, h=h, d=d, pb=pb: e.matmul(
                            out=pb[:], lhsT=qg[:, (2 * h + d) * 128:(2 * h + d + 1) * 128],
                            rhs=St[:, (2 * h + d) * 512:(2 * h + d + 1) * 512],
                            start=False, stop=(d == 1)))
                    P.mm_group(ems, rd=["scm", "qg", "vt%d" % c,
                                        "S%d" % (2 * h), "S%d" % (2 * h + 1)], wr=[bank])
                    P.op("scalar", lambda e, pb=pb, h=h: e.activation(
                        out=sqs[:, 0:512], in_=pb[:], func=AF.Square,
                        accum_out=cols[:, C_SSQ + h:C_SSQ + h + 1]),
                        rd=[bank], wr=["sqs", "cols"])
                P.op("scalar", lambda e: e.activation(
                    out=cols[:, C_R:C_R + 4], in_=cols[:, C_SSQ:C_SSQ + 4],
                    func=AF.Ln, scale=1.0 / 512.0, bias=cols[:, C_EPS:C_EPS + 1]),
                    rd=["cols"], wr=["cols"])
                P.op("scalar", lambda e: e.activation(
                    out=cols[:, C_R:C_R + 4], in_=cols[:, C_R:C_R + 4],
                    func=AF.Exp, scale=-0.5), rd=["cols"], wr=["cols"])
                for h in range(4):
                    bank = obank[h]
                    pb = halves[bank]
                    P.op("vector", lambda e, pb=pb, h=h, c=c: e.scalar_tensor_tensor(
                        out=og[:, h * 512:(h + 1) * 512], in0=pb[:],
                        scalar=cols[:, C_R + h:C_R + h + 1], op0=ALU.mult,
                        in1=gs[:, c * 2048 + h * 512:c * 2048 + (h + 1) * 512],
                        op1=ALU.mult), rd=[bank, "cols", "gs%d" % c], wr=["og"])
                if STAGE < 8:
                    continue
                for h in range(4):
                    for d in range(2):
                        sb = "po0" if (h * 2 + d) % 2 == 0 else "po1"
                        pb = halves[sb]
                        sk = "S%d" % (2 * h + d)
                        dd = 2 * h + d
                        P.mm_group([
                            (lambda e, pb=pb, sk=sk, dd=dd: e.matmul(
                                out=pb[:], lhsT=diagd[:, dd * 128:(dd + 1) * 128],
                                rhs=St[:, dd * 512:(dd + 1) * 512],
                                start=True, stop=False)),
                            (lambda e, pb=pb, h=h, d=d, c=c: e.matmul(
                                out=pb[:], lhsT=kt[:, (2 * h + d) * 128:(2 * h + d + 1) * 128],
                                rhs=vt[:, c * 2048 + h * 512:c * 2048 + (h + 1) * 512],
                                start=False, stop=True))],
                            rd=["diagd", "kt", "vt%d" % c, sk], wr=[sb])
                        P.op("vector", lambda e, pb=pb, h=h, d=d: e.tensor_copy(
                            out=St[:, (2 * h + d) * 512:(2 * h + d + 1) * 512],
                            in_=pb[:]), rd=[sb], wr=[sk])
                if STAGE < 9:
                    continue
                for g2 in range(2):
                    P.mm_group([
                        (lambda e, b=b, g2=g2: e.transpose(
                            out=pt[:, b * 128:(b + 1) * 128],
                            in_=og[:, g2 * 1024 + b * 128:g2 * 1024 + (b + 1) * 128],
                            identity=id128[:]))
                        for b in range(8)], rd=["og", "id128"], wr=["pt"])
                    for j in range(4):
                        pv = pt[:].rearrange("p (b c) -> p b c", c=128)[:, j::4, :]
                        xv2 = xT[:, g2 * 1024:(g2 + 1) * 1024].rearrange(
                            "p (b c) -> p b c", c=128)[:, j::4, :]
                        P.op("vector", lambda e, pv=pv, xv2=xv2, j=j: e.tensor_scalar(
                            out=xv2, in0=pv, scalar1=cols[:, C_GNW + j:C_GNW + j + 1],
                            op0=ALU.mult, scalar2=None),
                            rd=["pt", "cols"], wr=["xT"])
                if STAGE < 10:
                    continue
                for oh in range(2):
                    bank = "pr0" if oh == 0 else "pr1"
                    pb = halves[bank]
                    ems = []
                    for dvk in range(16):
                        wsl = ar[(oh * 2) + (dvk // 8)]
                        rc = (dvk % 8) * 512
                        ems.append(lambda e, pb=pb, wsl=wsl, rc=rc, dvk=dvk: e.matmul(
                            out=pb[:], lhsT=xT[:, dvk * 128:(dvk + 1) * 128],
                            rhs=wsl[:, rc:rc + 512], start=(dvk == 0), stop=(dvk == 15)))
                    P.mm_group(ems, rd=["xT"] + ko, wr=[bank])
                P.op("vector", lambda e, c=c: e.scalar_tensor_tensor(
                    out=x1r[:], in0=pr[:], scalar=1.0, op0=ALU.mult,
                    in1=xtm[:, c * 1024:(c + 1) * 1024], op1=ALU.add,
                    accum_out=cols[:, C_LN:C_LN + 1]),
                    rd=["pr0", "pr1", "xtm", ], wr=["x1r", "cols"])
                P.op("scalar", lambda e: e.activation(
                    out=sqs[:, 0:1024], in_=x1r[:], func=AF.Square,
                    accum_out=cols[:, C_LN + 1:C_LN + 2]),
                    rd=["x1r"], wr=["sqs", "cols"])
                P.op("vector", lambda e: e.tensor_scalar(
                    out=cols[:, C_LN + 2:C_LN + 3], in0=cols[:, C_LN:C_LN + 1],
                    scalar1=1.0 / 1024.0, op0=ALU.mult, scalar2=None),
                    rd=["cols"], wr=["cols"])
                P.op("vector", lambda e: e.tensor_mul(
                    cols[:, C_LN + 3:C_LN + 4], cols[:, C_LN + 2:C_LN + 3],
                    cols[:, C_LN + 2:C_LN + 3]), rd=["cols"], wr=["cols"])
                P.op("vector", lambda e: e.tensor_scalar(
                    out=cols[:, C_LN + 4:C_LN + 5], in0=cols[:, C_LN + 1:C_LN + 2],
                    scalar1=1.0 / 1024.0, op0=ALU.mult,
                    scalar2=cols[:, C_LN + 3:C_LN + 4], op1=ALU.subtract),
                    rd=["cols"], wr=["cols"])
                P.op("scalar", lambda e: e.activation(
                    out=cols[:, C_LN + 5:C_LN + 6], in_=cols[:, C_LN + 4:C_LN + 5],
                    func=AF.Ln, bias=cols[:, C_EPS:C_EPS + 1]), rd=["cols"], wr=["cols"])
                P.op("scalar", lambda e: e.activation(
                    out=cols[:, C_LN + 5:C_LN + 6], in_=cols[:, C_LN + 5:C_LN + 6],
                    func=AF.Exp, scale=-0.5), rd=["cols"], wr=["cols"])
                P.op("vector", lambda e: e.tensor_scalar(
                    out=x1c[:], in0=x1r[:], scalar1=cols[:, C_LN + 2:C_LN + 3],
                    op0=ALU.subtract, scalar2=cols[:, C_LN + 5:C_LN + 6],
                    op1=ALU.mult), rd=["x1r", "cols"], wr=["x1c"])
                xbslice = x1bf[:, (c % 2) * 1024:(c % 2) * 1024 + 1024]
                P.op("vector", lambda e, xbslice=xbslice: e.tensor_scalar(
                    out=xbslice, in0=x1r[:], scalar1=cols[:, C_LN + 2:C_LN + 3],
                    op0=ALU.subtract, scalar2=cols[:, C_LN + 5:C_LN + 6],
                    op1=ALU.mult), rd=["x1r", "cols"], wr=["x1bf%d" % (c % 2)])
                P.dma("gpsimd", lambda e, t=t, c=c: e.dma_start(
                    out=x1v[:, 4 * t + c, :], in_=x1c[:]), rd=["x1c"], wr=["x1_s"],
                    chan="x1st", append=True)
                def _x1T(c=c, xbslice=xbslice):
                    P.mm_group([
                        (lambda e, kk=kk: e.transpose(
                            out=pt[:, kk * 128:(kk + 1) * 128],
                            in_=xbslice[:, kk * 128:(kk + 1) * 128], identity=id128[:]))
                        for kk in range(8)], rd=["x1bf%d" % (c % 2), "id128"], wr=["pt"])
                    xts = x1Tt[:, (c % 2) * 1024:(c % 2) * 1024 + 1024]
                    P.op("vector", lambda e, xts=xts: e.tensor_copy(
                        out=xts, in_=pt[:]), rd=["pt"], wr=["x1Tt%d" % (c % 2)])
                    P.dma("gpsimd", lambda e, t=t, c=c, xts=xts: e.dma_start(
                        out=x1T_s[:, t * 4096:(t + 1) * 4096].rearrange(
                            "p (k v) -> p k v", v=512)[:, :, c * 128:(c + 1) * 128],
                        in_=xts.rearrange("p (k v) -> p k v", v=128)),
                        rd=["x1Tt%d" % (c % 2)], wr=["x1T_s%d" % (c % 2)],
                        chan="x1Tst%d" % (c % 2), append=True)
                pend.append(_x1T)
            while pend:
                pend.pop(0)()

        P.flush_bg()

        if not skip_b:
            P.inherit(H1K + AR, ["vt%d" % j for j in range(4)] +
                      ["gs%d" % j for j in range(4)] + AR + ["xT"])
            for t in range(n_b_tiles):
                P.dma("gpsimd", lambda e, t=t: e.dma_start(
                    out=qf[:], in_=x1T_s[:, t * 4096:(t + 1) * 4096]),
                    rd=["x1T_s0", "x1T_s1"], wr=["qfb"], chan="x1Tld")
                P.dma("gpsimd", lambda e, t=t: e.dma_start(
                    out=xtm[:].rearrange("p (n c) -> p n c", c=1024),
                    in_=x1v[:, 4 * t:4 * t + 4, :]), rd=["x1_s"], wr=["xtm"],
                    chan="x1ld")
                P.mm_group([
                    (lambda e, kk=kk: e.matmul(
                        out=pu[0:16, :], lhsT=afcT[:, kk * 16:(kk + 1) * 16],
                        rhs=qf[:, kk * 512:(kk + 1) * 512],
                        start=(kk == 0), stop=(kk == 7)))
                    for kk in range(8)], rd=["afcT", "qfb"], wr=["pu"])
                P.op("vector", lambda e: e.tensor_copy(out=za[0:16, :], in_=pu[0:16, :]),
                     rd=["pu"], wr=["za"])
                for oc in range(8):
                    slot = ar[oc % 4]
                    skey = AR[oc % 4]
                    qe = "sync" if oc % 2 == 0 else "gpsimd"
                    P.dma(qe, lambda e, oc=oc, slot=slot: e.dma_start(
                        out=slot, in_=wfc_s[:, oc * 4096:(oc + 1) * 4096]),
                        rd=["wfc_s0", "wfc_s1"], wr=[skey], chan=skey)
                    for o4 in range(4):
                        o = oc * 4 + o4
                        bank = qproj_rot[o % 2]
                        pb = halves[bank]
                        ems = []
                        for kk in range(8):
                            ems.append(lambda e, pb=pb, slot=slot, o4=o4, kk=kk: e.matmul(
                                out=pb[:], lhsT=slot[:, o4 * 1024 + kk * 128:o4 * 1024 + (kk + 1) * 128],
                                rhs=qf[:, kk * 512:(kk + 1) * 512],
                                start=(kk == 0), stop=False))
                        ems.append(lambda e, pb=pb, o=o: e.matmul(
                            out=pb[:], lhsT=fca[:, o * 128:(o + 1) * 128],
                            rhs=za[:], start=False, stop=True))
                        P.mm_group(ems, rd=[skey, "qfb", "fca", "za"], wr=[bank])
                        P.op("scalar", lambda e, pb=pb, o=o: e.activation(
                            out=h1[:, o * 512:(o + 1) * 512], in_=pb[:], func=AF.Gelu,
                            scale=cols[:, C_SF:C_SF + 1]),
                            rd=[bank, "cols"], wr=["h1_%d" % o])
                P.mm_group([
                    (lambda e, kk=kk: e.matmul(
                        out=pu[0:16, :], lhsT=apjT[:, kk * 16:(kk + 1) * 16],
                        rhs=h1[:, kk * 512:(kk + 1) * 512],
                        start=(kk == 0), stop=(kk == 31)))
                    for kk in range(32)], rd=H1K + ["apjT"], wr=["pu"])
                P.op("vector", lambda e: e.tensor_copy(out=z2a[0:16, :], in_=pu[0:16, :]),
                     rd=["pu"], wr=["z2a"])
                for o in range(8):
                    slot = ar[o % 4]
                    skey = AR[o % 4]
                    qe = "sync" if o % 2 == 0 else "gpsimd"
                    P.dma(qe, lambda e, o=o, slot=slot: e.dma_start(
                        out=slot, in_=wpj_s[:, o * 4096:(o + 1) * 4096]),
                        rd=["wpj_s0", "wpj_s1"], wr=[skey], chan=skey)
                    bank = ["po0", "po1"][o % 2]
                    pb = halves[bank]
                    ems = []
                    for kk in range(32):
                        ems.append(lambda e, pb=pb, slot=slot, kk=kk: e.matmul(
                            out=pb[:], lhsT=slot[:, kk * 128:(kk + 1) * 128],
                            rhs=h1[:, kk * 512:(kk + 1) * 512],
                            start=(kk == 0), stop=False))
                    ems.append(lambda e, pb=pb, o=o: e.matmul(
                        out=pb[:], lhsT=pja[:, o * 128:(o + 1) * 128], rhs=z2a[:],
                        start=False, stop=True))
                    P.mm_group(ems, rd=[skey, "z2a", "pja"] + H1K, wr=[bank])
                    P.op("vector", lambda e, pb=pb, o=o: e.tensor_copy(
                        out=x2f32[:, o * 512:(o + 1) * 512], in_=pb[:]),
                        rd=[bank], wr=["x2f32"])
                for j in range(4):
                    for hb in range(2):
                        bank = "pq0" if hb == 0 else "pq1"
                        pb = halves[bank]
                        P.mm_group([
                            (lambda e, pb=pb, o4=o4, hb=hb, j=j: e.transpose(
                                out=pb[:, o4 * 128:(o4 + 1) * 128],
                                in_=x2f32[:, (hb * 4 + o4) * 512 + j * 128:
                                          (hb * 4 + o4) * 512 + (j + 1) * 128],
                                identity=idf32[:]))
                            for o4 in range(4)], rd=["x2f32", "idf32"], wr=[bank])
                    P.op("vector", lambda e, j=j: e.scalar_tensor_tensor(
                        out=x1r[:], in0=pq[:], scalar=cols[:, C_SP:C_SP + 1],
                        op0=ALU.mult, in1=xtm[:, j * 1024:(j + 1) * 1024], op1=ALU.add,
                        accum_out=cols[:, C_LN:C_LN + 1]),
                        rd=["pq0", "pq1", "xtm", "cols"], wr=["x1r", "cols"])
                    P.op("scalar", lambda e: e.activation(
                        out=sqs[:, 0:1024], in_=x1r[:], func=AF.Square,
                        accum_out=cols[:, C_LN + 1:C_LN + 2]),
                        rd=["x1r"], wr=["sqs", "cols"])
                    P.op("vector", lambda e: e.tensor_scalar(
                        out=cols[:, C_LN + 2:C_LN + 3], in0=cols[:, C_LN:C_LN + 1],
                        scalar1=1.0 / 1024.0, op0=ALU.mult, scalar2=None),
                        rd=["cols"], wr=["cols"])
                    P.op("vector", lambda e: e.tensor_mul(
                        cols[:, C_LN + 3:C_LN + 4], cols[:, C_LN + 2:C_LN + 3],
                        cols[:, C_LN + 2:C_LN + 3]), rd=["cols"], wr=["cols"])
                    P.op("vector", lambda e: e.tensor_scalar(
                        out=cols[:, C_LN + 4:C_LN + 5], in0=cols[:, C_LN + 1:C_LN + 2],
                        scalar1=1.0 / 1024.0, op0=ALU.mult,
                        scalar2=cols[:, C_LN + 3:C_LN + 4], op1=ALU.subtract),
                        rd=["cols"], wr=["cols"])
                    P.op("scalar", lambda e: e.activation(
                        out=cols[:, C_LN + 5:C_LN + 6], in_=cols[:, C_LN + 4:C_LN + 5],
                        func=AF.Ln, bias=cols[:, C_EPS:C_EPS + 1]), rd=["cols"], wr=["cols"])
                    P.op("scalar", lambda e: e.activation(
                        out=cols[:, C_LN + 5:C_LN + 6], in_=cols[:, C_LN + 5:C_LN + 6],
                        func=AF.Exp, scale=-0.5), rd=["cols"], wr=["cols"])
                    P.op("vector", lambda e: e.tensor_scalar(
                        out=x1c[:], in0=x1r[:], scalar1=cols[:, C_LN + 2:C_LN + 3],
                        op0=ALU.subtract, scalar2=cols[:, C_LN + 5:C_LN + 6],
                        op1=ALU.mult), rd=["x1r", "cols"], wr=["x1c"])
                    P.dma("gpsimd", lambda e, t=t, j=j: e.dma_start(
                        out=yv[:, 4 * t + j, :], in_=x1c[:]), rd=["x1c"], wr=["y_h"],
                        chan="yst", append=True)

        for ename in ENGINES:
            P.barrier(ename)

        import os as _os
        if _os.environ.get("PROBE_SBUF"):
            lo, hi = 0, 131072
            pidx = 0
            while lo < hi:
                mid = (lo + hi + 1) // 2
                pidx += 1
                try:
                    with nc.sbuf_tensor("probe%d" % pidx, [128, mid], mybir.dt.int8):
                        lo = mid
                except AssertionError:
                    hi = mid - 1
            print("SBUF headroom bytes/partition:", lo)
        block = es.enter_context(nc.Block())
        P.emit(nc, es, block)

    return nc


def host_constants():
    import ml_dtypes
    bf = ml_dtypes.bfloat16
    kb = np.zeros((128, 512), dtype=bf)
    kb[:, 0:128] = np.eye(128, dtype=np.float32).astype(bf)
    jj, ii = np.meshgrid(np.arange(128), np.arange(128), indexing="ij")
    kb[:, 128:256] = np.where(jj <= ii, np.float32(-1.0 / 16.0), 0.0).astype(bf)
    kb[:, 256:384] = np.where(jj > ii, np.float32(-1.0 / 16.0), 0.0).astype(bf)
    kb[:, 384:512] = np.where(jj <= ii, np.float32(1.0), 0.0).astype(bf)
    kfc = np.zeros((128, 8), dtype=np.float32)
    kfc[:, 0] = 1.5
    kfc[:, 1] = NEGLN16
    kfc[:, 2] = EPS
    return kb, kfc


def kernel(**inputs):
    kb, kfc = host_constants()
    nc = build_program()
    names = {
        "Wq": "L0_attn_Wq", "Wk": "L0_attn_Wk", "Wv": "L0_attn_Wv",
        "Wg": "L0_attn_Wg", "Wgk1": "L0_attn_Wgk1", "Wgk2": "L0_attn_Wgk2",
        "bgk2": "L0_attn_bgk2", "gnw": "L0_attn_g_norm_w", "Wo": "L0_attn_Wo",
        "Wfc": "L0_ffn_W_fc", "bfc": "L0_ffn_b_fc", "Afc": "L0_ffn_A_fc",
        "Bfc": "L0_ffn_B_fc", "Wpj": "L0_ffn_W_proj", "bpj": "L0_ffn_b_proj",
        "Apj": "L0_ffn_A_proj", "Bpj": "L0_ffn_B_proj",
    }
    x = np.ascontiguousarray(inputs["L0_attn_hidden_states"])
    in_maps = []
    for core in range(8):
        m = {"xin": x[2 * core:2 * core + 2].reshape(NTOK, D), "kb": kb, "kfc": kfc}
        for k, nm in names.items():
            a = inputs[nm]
            if a.ndim == 1:
                a = a.reshape(1, -1)
            m[k] = np.ascontiguousarray(a)
        in_maps.append(m)
    res = run_bass_kernel_spmd(nc, in_maps, core_ids=list(range(8)))
    out = np.concatenate([res.results[c]["y"].reshape(2, 2048, D) for c in range(8)],
                         axis=0)
    return out.astype(np.float32)
